# version 11 (final)
# baseline (speedup 1.0000x reference)
import numpy as np
from contextlib import ExitStack

import concourse.bass as bass
import concourse.mybir as mybir
from concourse.bass_utils import run_bass_kernel_spmd

F32 = mybir.dt.float32
BF16 = mybir.dt.bfloat16
AF = mybir.ActivationFunctionType
ALU = mybir.AluOpType

B, T, D = 4, 4096, 2048
H, HD, HHD = 16, 128, 64
HL = 8
SCALE = 0.125
EBIAS = -40.0
HID = 8192
TOK = 2048
NB = T // 128
LN_EPS = 1e-5
NEG = -1.0e30
FFN_TP = [512, 512, 512, 512]
ENGINES = ("tensor", "scalar", "vector", "gpsimd", "sync")

DEBUG_TAPS = []
PHASE_LIMIT = 4
SKIP_COLLECTIVE = False
DEBUG_FIN = None
P2_TRUNC = None
P1_BLOCKS = 8


class Sem:
    def __init__(self, stack, nc, name):
        self.h = stack.enter_context(nc.semaphore(name))
        self.n = 0
        self.name = name


class Phase:

    def __init__(self, nc):
        self.nc = nc
        self.ops = []

    def op(self, eng, fn, waits=(), incs=()):
        waits = [(s, v) for (s, v) in waits]
        incs = [(s, k) for (s, k) in incs]
        for s, k in incs:
            s.n += k
        self.ops.append((eng, fn, waits, incs))

    def wait(self, eng, sem, val):
        self.ops.append((eng, None, [(sem, val)], []))

    def barrier_inc(self, sbar):
        for e in ENGINES:
            self.ops.append((e, "BARINC", [], [(sbar, 1)]))
        sbar.n += 0

    def barrier_wait(self, sbar, val):
        for e in ENGINES:
            self.ops.append((e, None, [(sbar, val)], []))

    def emit(self):
        with self.nc.Block() as block:
            def run(eng_name, eng):
                for e, fn, waits, incs in self.ops:
                    if e != eng_name:
                        continue
                    for s, v in waits:
                        eng.wait_ge(s.h, v)
                    if fn is None:
                        continue
                    if fn == "BARINC":
                        for s, k in incs:
                            eng.sem_inc(s.h, k)
                        continue
                    ins = fn(eng)
                    for s, k in incs:
                        ins.then_inc(s.h, k)

            @block.tensor
            def _(t):
                run("tensor", t)

            @block.scalar
            def _(sc):
                run("scalar", sc)

            @block.vector
            def _(v):
                run("vector", v)

            @block.gpsimd
            def _(gp):
                run("gpsimd", gp)

            @block.sync
            def _(sy):
                run("sync", sy)


def ssync(eng, sem, ins):
    sem.n += 1
    ins.then_inc(sem.h, 1)
    eng.wait_ge(sem.h, sem.n)


class Ctx:
    pass


def build(nc, debug_taps=(), phase_limit=5):
    c = Ctx()
    c.nc = nc
    dt = nc.dram_tensor
    c.xT = dt("xT", [D, T], F32, kind="ExternalInput")
    c.wq = dt("wq", [D, HL * HD], F32, kind="ExternalInput")
    c.wk = dt("wk", [D, HL * HD], F32, kind="ExternalInput")
    c.wv = dt("wv", [D, HL * HD], F32, kind="ExternalInput")
    c.wo = dt("wo", [H * HD, D], F32, kind="ExternalInput")
    c.lam = dt("lam", [1, HL], F32, kind="ExternalInput")
    c.wgT = dt("wgT", [D, HID], F32, kind="ExternalInput")
    c.wuT = dt("wuT", [D, HID], F32, kind="ExternalInput")
    c.wdT = dt("wdT", [HID, D], F32, kind="ExternalInput")
    c.maskc = dt("maskc", [128, 128], F32, kind="ExternalInput")
    c.ones_row = dt("ones_row", [1, 128], F32, kind="ExternalInput")
    c.ones_col = dt("ones_col", [128, 1], F32, kind="ExternalInput")
    c.idt = dt("idt", [128, 128], F32, kind="ExternalInput")
    c.xresT = dt("xresT", [D, TOK], F32, kind="ExternalInput")
    c.cvec = dt("cvec", [128, 2], F32, kind="ExternalInput")
    c.out = dt("out", [D, TOK], F32, kind="ExternalOutput")
    c.Kd = dt("Kd", [HL * HD, T], BF16)
    c.Qd = dt("Qd", [HL * HD, T], BF16)
    c.Vd = dt("Vd", [NB, 128, HL * 129], BF16)
    c.ACCd = [dt(f"ACCd{g}", [T, 4 * HD], BF16) for g in range(2)]
    c.ACCg = [dt(f"ACCg{g}", [2 * T, 4 * HD], BF16) for g in range(2)]
    c.ACCl = dt("ACCl", [TOK, H * HD], BF16)
    c.Y1f = dt("Y1f", [D, TOK], F32)
    c.Y1b = dt("Y1b", [D, TOK], BF16)
    c.Y2f = dt("Y2f", [D, TOK], F32)
    c.wgB = dt("wgB", [64, 128, 2048], BF16)
    c.wuB = dt("wuB", [64, 128, 2048], BF16)
    c.wdB = dt("wdB", [16, 128, 8192], BF16)

    with ExitStack() as top:
        E = top.enter_context
        c.lamneg = E(nc.sbuf_tensor("lamneg", [128, HL], F32))
        c.idt_bf = E(nc.sbuf_tensor("idt_bf", [128, 128], BF16))
        c.mask_sb = E(nc.sbuf_tensor("mask_sb", [128, 128], F32))
        c.onesr = E(nc.sbuf_tensor("onesr", [1, 128], F32))
        c.onesc = E(nc.sbuf_tensor("onesc", [128, 1], F32))
        c.cvec_sb = E(nc.sbuf_tensor("cvec_sb", [128, 2], F32))
        for nm in ("sBAR", "sW", "sX", "sPE", "sEVK", "sEVV", "sSTK", "sSTV",
                   "sMISC", "s2L", "s2PS", "s2MK", "s2EX", "s2AV", "s2FI",
                   "s2AS", "sCC", "s3L", "s3XR", "s3WO", "s3TP", "s3DV",
                   "s3EO", "s3SQ", "s3ST", "s3SM", "s3AC", "s3MM", "s3Y1",
                   "s3SO", "s3SB", "s4YL", "s4WL", "s4PE", "s4AC", "s4DV",
                   "s4WL2", "s4PE2", "s4RL", "s4DV2", "s4SO", "s5L", "sVL", "s5SQ",
                   "s5ST", "s5SM", "s5AC", "s5MM", "s5Y", "s5SO", "sVS",
                   "sSS", "sLC", "s2FB", "s4WP"):
            setattr(c, nm, Sem(top, nc, nm))
        c.sXs = [Sem(top, nc, f"sXs{i}") for i in range(2)]
        c.sSTKs = [Sem(top, nc, f"sSTKs{i}") for i in range(4)]
        c.sSTVs = [Sem(top, nc, f"sSTVs{i}") for i in range(2)]
        c.s2ASs = [Sem(top, nc, f"s2ASs{i}") for i in range(4)]
        c.s3Ls = [Sem(top, nc, f"s3Ls{i}") for i in range(2)]
        c.s3LBs = [Sem(top, nc, f"s3LBs{i}") for i in range(2)]
        c.s3XRs = [Sem(top, nc, f"s3XRs{i}") for i in range(2)]
        c.s4WLs = [Sem(top, nc, f"s4WLs{i}") for i in range(2)]
        c.s4WL2s = [Sem(top, nc, f"s4WL2s{i}") for i in range(2)]
        c.s4RLs = [Sem(top, nc, f"s4RLs{i}") for i in range(2)]
        c.s4SOs = [Sem(top, nc, f"s4SOs{i}") for i in range(2)]
        c.s4NLs = [Sem(top, nc, f"s4NLs{i}") for i in range(2)]

        phases = [_p1, _p2, _p3, _p4]
        for i in range(min(phase_limit, 4)):
            phases[i](c, top)
        _taps(c, phase_limit, debug_taps)
    return nc


def _taps(c, nphases, debug_taps):
    if not debug_taps:
        return
    nc = c.nc
    P = Phase(nc)
    P.barrier_wait(c.sBAR, 5 * nphases)
    sT = c.s5SO
    for nm in debug_taps:
        src = getattr(c, nm)

        def ftap(gp, src=src, nm=nm):
            tout = nc.dram_tensor(f"tap_{nm}", list(src.ap().shape),
                                  F32, kind="ExternalOutput")
            return gp.dma_start(out=tout.ap(), in_=src.ap())
        P.op("gpsimd", ftap, incs=[(sT, 16)])
    P.wait("gpsimd", sT, sT.n)
    P.emit()


def _p1(c, top):
    nc = c.nc
    with ExitStack() as st:
        E = st.enter_context
        wq_sb = E(nc.sbuf_tensor("wq_sb", [128, 16 * 1024], BF16))
        wk_sb = E(nc.sbuf_tensor("wk_sb", [128, 16 * 1024], BF16))
        wv_sb = E(nc.sbuf_tensor("wv_sb", [128, 16 * 1024], BF16))
        xb = [E(nc.sbuf_tensor(f"xb{i}", [128, 16 * 512], BF16)) for i in range(2)]
        kq_st = [E(nc.sbuf_tensor(f"kqst{i}", [128, 512], BF16)) for i in range(4)]
        v_st = [E(nc.sbuf_tensor(f"vst{i}", [128, 4 * 1032], BF16)) for i in range(2)]
        lam_sb = E(nc.sbuf_tensor("lam_sb", [1, HL], F32))
        lam_t1 = E(nc.sbuf_tensor("lam_t1", [1, HL], F32))
        lam_t2 = E(nc.sbuf_tensor("lam_t2", [1, HL], F32))
        psp = [E(nc.psum_tensor(f"psp{i}", [128, 512], F32)) for i in range(4)]
        ps_lam = E(nc.psum_tensor("ps_lam", [128, HL], F32))

        P = Phase(nc)
        W_SB = {"q": wq_sb, "k": wk_sb, "v": wv_sb}
        KD = {"q": c.Qd, "k": c.Kd}

        def wload(dst, src):
            return lambda g: g.dma_start(
                out=dst.ap().rearrange("p (c h) -> p c h", c=16),
                in_=src.ap().rearrange("(c p) h -> p c h", p=128))
        P.op("gpsimd", wload(wq_sb, c.wq), incs=[(c.sW, 16)])
        P.op("gpsimd", wload(wk_sb, c.wk), incs=[(c.sW, 16)])
        P.op("gpsimd", wload(wv_sb, c.wv), incs=[(c.sW, 16)])
        P.op("gpsimd", lambda g: g.dma_start(out=c.idt_bf[:, :], in_=c.idt[:, :]),
             incs=[(c.sW, 16)])
        P.op("gpsimd", lambda g: g.dma_start(out=c.mask_sb[:, :], in_=c.maskc[:, :]),
             incs=[(c.sW, 16)])
        P.op("gpsimd", lambda g: g.dma_start(out=c.onesr[:, :], in_=c.ones_row[:, :]),
             incs=[(c.sW, 16)])
        P.op("gpsimd", lambda g: g.dma_start(out=c.onesc[:, :], in_=c.ones_col[:, :]),
             incs=[(c.sW, 16)])
        P.op("gpsimd", lambda g: g.dma_start(out=lam_sb[:, :], in_=c.lam[:, :]),
             incs=[(c.sW, 16)])
        P.op("gpsimd", lambda g: g.dma_start(out=c.cvec_sb[:, :], in_=c.cvec[:, :]),
             incs=[(c.sW, 16)])
        WTOT = c.sW.n

        P.op("vector", lambda v, s=0: v.memset(v_st[s].ap(), 1.0), incs=[(c.sMISC, 1)])
        P.op("vector", lambda v, s=1: v.memset(v_st[s].ap(), 1.0), incs=[(c.sMISC, 1)])
        VONES = c.sMISC.n

        P.op("scalar", lambda s: s.activation(lam_t1[:, :], lam_sb[:, :], AF.Exp,
                                              scale=-1.0),
             waits=[(c.sW, WTOT)], incs=[(c.sMISC, 1)])
        P.op("vector", lambda v: v.tensor_scalar_add(lam_t2[:, :], lam_t1[:, :], 1.0),
             waits=[(c.sMISC, VONES + 1)], incs=[(c.sMISC, 1)])
        P.op("vector", lambda v: v.reciprocal(lam_t1[:, :], lam_t2[:, :]),
             waits=[(c.sMISC, c.sMISC.n)], incs=[(c.sMISC, 1)])
        P.op("vector", lambda v: v.tensor_scalar_mul(lam_t2[:, :], lam_t1[:, :], -1.0),
             waits=[(c.sMISC, c.sMISC.n)], incs=[(c.sMISC, 1)])
        LAMR = c.sMISC.n
        P.op("tensor", lambda t: t.matmul(ps_lam[:, :], c.onesr[:, :], lam_t2[:, :],
                                          start=True, stop=True),
             waits=[(c.sMISC, LAMR)], incs=[(c.sMISC, 1)])
        P.op("vector", lambda v: v.tensor_copy(c.lamneg[:, :], ps_lam[:, :]),
             waits=[(c.sMISC, c.sMISC.n)], incs=[(c.sMISC, 1)])

        uses = ([("k", hl) for hl in range(HL)] + [("q", hl) for hl in range(HL)]
                + [("v", j2, hc) for j2 in range(4) for hc in range(2)])
        NU = len(uses)
        evac_of = {}
        kq_i = 0
        for bi in range(P1_BLOCKS):
            def xload(g, bi=bi):
                return g.dma_start(
                    out=xb[bi % 2].ap().rearrange("p (cc t) -> p cc t", cc=16),
                    in_=c.xT.ap()[:, bi * 512:(bi + 1) * 512]
                        .rearrange("(cc p) t -> p cc t", p=128))
            xw = []
            if bi >= 2:
                xw.append((c.sPE, NU * (bi - 1)))
            P.op("gpsimd", xload, waits=xw, incs=[(c.sXs[bi % 2], 16)])

            for ui, use in enumerate(uses):
                idx = bi * NU + ui
                ps = psp[idx % 4]
                mmw = [(c.sXs[bi % 2], 16 * (bi // 2 + 1))]
                if bi == 0 and ui == 0:
                    mmw.append((c.sW, WTOT))
                if idx >= 4:
                    mmw.append(evac_of[idx - 4])

                if use[0] in ("k", "q"):
                    kind, hl = use

                    def mm(t, ps=ps, kind=kind, hl=hl, x=xb[bi % 2]):
                        w = W_SB[kind]
                        for cc in range(16):
                            ins = t.matmul(
                                ps[:, :],
                                w[:, cc * 1024 + hl * 128:cc * 1024 + hl * 128 + 128],
                                x[:, cc * 512:(cc + 1) * 512],
                                start=(cc == 0), stop=(cc == 15))
                        return ins
                    P.op("tensor", mm, waits=mmw, incs=[(c.sPE, 1)])
                    stg = kq_st[kq_i % 4]
                    evw = [(c.sPE, idx + 1)]
                    if kq_i >= 4:
                        evw.append((c.sSTKs[kq_i % 4], 16 * (kq_i // 4)))

                    def ev(s, stg=stg, ps=ps):
                        return s.copy(stg[:, :], ps[:, :])
                    P.op("scalar", ev, waits=evw, incs=[(c.sEVK, 1)])
                    evac_of[idx] = (c.sEVK, c.sEVK.n)
                    dram = KD[kind]

                    def stf(sy, dram=dram, hl=hl, bi=bi, stg=stg):
                        return sy.dma_start(
                            out=dram.ap()[hl * 128:(hl + 1) * 128,
                                          bi * 512:(bi + 1) * 512],
                            in_=stg[:, :])
                    P.op("sync", stf, waits=[(c.sEVK, c.sEVK.n)],
                         incs=[(c.sSTKs[kq_i % 4], 16)])
                    kq_i += 1
                else:
                    _, j2, hc = use

                    def mm(t, ps=ps, j2=j2, hc=hc, x=xb[bi % 2]):
                        for cc in range(16):
                            ins = t.matmul(
                                ps[:, :],
                                x[:, cc * 512 + j2 * 128:cc * 512 + j2 * 128 + 128],
                                wv_sb[:, cc * 1024 + hc * 512:cc * 1024 + (hc + 1) * 512],
                                start=(cc == 0), stop=(cc == 15))
                        return ins
                    P.op("tensor", mm, waits=mmw, incs=[(c.sPE, 1)])
                    stg = v_st[bi % 2]
                    evw = [(c.sPE, idx + 1)]
                    if bi >= 2 and ui == 16:
                        evw.append((c.sSTVs[bi % 2], 16 * (bi // 2)))

                    def ev(v, stg=stg, ps=ps, j2=j2, hc=hc):
                        dst = stg.ap().rearrange("p (j h d) -> p j h d", j=4, h=HL)
                        dst = dst[:, j2, hc * 4:(hc + 1) * 4, 0:128]
                        return v.tensor_copy(
                            dst, ps.ap().rearrange("p (h d) -> p h d", h=4))
                    P.op("vector", ev, waits=evw, incs=[(c.sEVV, 1)])
                    evac_of[idx] = (c.sEVV, c.sEVV.n)

            def vstf(sy, bi=bi, stg=v_st[bi % 2]):
                return sy.dma_start(
                    out=c.Vd.ap()[4 * bi:4 * bi + 4, :, :]
                        .rearrange("j p h -> p j h"),
                    in_=stg.ap().rearrange("p (j h) -> p j h", j=4))
            P.op("sync", vstf, waits=[(c.sEVV, c.sEVV.n)],
                 incs=[(c.sSTVs[bi % 2], 16)])

        for sm in c.sSTKs + c.sSTVs:
            P.wait("sync", sm, sm.n)
        P.barrier_inc(c.sBAR)
        P.emit()


def _p2(c, top):
    nc = c.nc
    with ExitStack() as st:
        E = st.enter_context
        k_sb = E(nc.sbuf_tensor("k_sb", [128, 4 * T], BF16))
        q_sb = E(nc.sbuf_tensor("q_sb", [128, 4 * T], BF16))
        v_sb = E(nc.sbuf_tensor("v_sb", [128, NB * 4 * 129], BF16))
        p_sb = [E(nc.sbuf_tensor(f"p_sb{i}", [128, 1024], BF16)) for i in range(4)]
        acc_st = [E(nc.sbuf_tensor(f"accst{i}", [128, 512], BF16)) for i in range(4)]
        fin_a1 = E(nc.sbuf_tensor("fin_a1", [128, 4 * 128], F32))
        fin_a2 = E(nc.sbuf_tensor("fin_a2", [128, 4 * 128], F32))
        fin_u = E(nc.sbuf_tensor("fin_u", [128, 4 * 128], F32))
        fr1 = E(nc.sbuf_tensor("fr1", [128, 4], F32))
        fr2 = E(nc.sbuf_tensor("fr2", [128, 4], F32))
        ft1 = E(nc.sbuf_tensor("ft1", [128, 4], F32))
        fkn = E(nc.sbuf_tensor("fkn", [128, 4], F32))
        psS = [E(nc.psum_tensor(f"psS{i}", [128, 1024], F32)) for i in range(2)]
        accP = E(nc.psum_tensor("accP", [128, 2048], F32))

        def a1col(stt):
            return (stt // 2) * 512 + (stt % 2) * 130

        def a2col(stt):
            return 1024 + a1col(stt)

        P = Phase(nc)
        P.barrier_wait(c.sBAR, 5)

        events = []
        runs = []
        for g in range(2):
            for ig in range(8):
                for hl in range(4):
                    if P2_TRUNC is not None and len(runs) >= P2_TRUNC:
                        continue
                    r = len(runs)
                    t0 = ig * 4
                    ev0 = len(events)
                    for j in range(t0 + 4):
                        events.append((g, ig, hl, j, r))
                    runs.append((g, ig, hl, ev0, len(events)))
        NEV = len(events)
        NRUN = len(runs)

        av_after_group = {}
        n = 0
        for g in range(2):
            for _, ig, hl, j, r in events:
                pass
        av_cnt = []
        tot = 0
        for (g, ig, hl, j, r) in events:
            tot += 1
            av_cnt.append(tot)
        g_end_ev = {0: 0, 1: 0}
        first_ev_of_g = {}
        for i, (g, ig, hl, j, r) in enumerate(events):
            g_end_ev[g] = i + 1
            if g not in first_ev_of_g:
                first_ev_of_g[g] = i
        ps_cnt = list(range(1, NEV + 1))
        ex_cnt = list(range(1, NEV + 1))
        mk_cnt = []
        mtot = 0
        for (g, ig, hl, j, r) in events:
            if j >= ig * 4:
                mtot += 1
            mk_cnt.append(mtot)

        def gloadKQ(g):
            def f(sy, g=g):
                sy.dma_start(
                    out=k_sb.ap().rearrange("p (h t) -> p h t", h=4),
                    in_=c.Kd.ap()[g * 512:(g + 1) * 512, :]
                        .rearrange("(h p) t -> p h t", p=128)
                ).then_inc(c.s2L.h, 16)
                return sy.dma_start(
                    out=q_sb.ap().rearrange("p (h t) -> p h t", h=4),
                    in_=c.Qd.ap()[g * 512:(g + 1) * 512, :]
                        .rearrange("(h p) t -> p h t", p=128))
            return f

        def gloadV(g):
            def f(sy, g=g):
                return sy.dma_start(
                    out=v_sb.ap().rearrange("p (b h) -> p b h", b=NB),
                    in_=c.Vd.ap()[:, :, g * 516:(g + 1) * 516]
                        .rearrange("b p h -> p b h"))
            return f

        def mk_scores(i):
            g, ig, hl, j, r = events[i]
            t0 = ig * 4
            diag = j >= t0
            co = 128 * max(0, j - t0)
            b = i % 2

            def f(t):
                t.matmul(psS[b][:, co:512],
                         k_sb[0:64, hl * T + j * 128:hl * T + j * 128 + 128],
                         q_sb[0:64, hl * T + ig * 512 + co:hl * T + (ig + 1) * 512],
                         start=not diag, stop=True, tile_position=(0, 0),
                         skip_group_check=diag)
                return t.matmul(
                    psS[b][:, 512 + co:1024],
                    k_sb[64:128, hl * T + j * 128:hl * T + j * 128 + 128],
                    q_sb[64:128, hl * T + ig * 512 + co:hl * T + (ig + 1) * 512],
                    start=not diag, stop=True, tile_position=(64, 0),
                    skip_group_check=diag)
            w = []
            if i >= 2:
                w.append((c.s2EX, i - 1))
            w.append((c.s2L, 32 * (g + 1)))
            if j >= t0:
                w.append((c.s2MK, mk_cnt[i]))
            return f, w

        def mk_mask(i):
            g, ig, hl, j, r = events[i]
            t0 = ig * 4
            sd = j - t0
            co = 128 * sd
            b = i % 2

            def f(v):
                ins = v.tensor_copy(psS[b][:, co:co + 128], c.mask_sb[:, :])
                ins = v.tensor_copy(psS[b][:, 512 + co:512 + co + 128],
                                    c.mask_sb[:, :])
                if co + 128 < 512:
                    ins = v.memset(psS[b][:, co + 128:512], 0.0)
                    ins = v.memset(psS[b][:, 512 + co + 128:1024], 0.0)
                return ins
            w = []
            if i >= 2:
                w.append((c.s2EX, i - 1))
            return f, w

        def mk_exp(i):
            b = i % 2
            g, ig, hl, j, r = events[i]
            t0 = ig * 4
            co = 128 * max(0, j - t0)

            def f(s):
                pv = psS[b].ap().rearrange("p (two n) -> p two n", two=2)[:, :, co:512]
                ov = p_sb[i % 4].ap().rearrange("p (two n) -> p two n",
                                                two=2)[:, :, co:512]
                return s.activation(ov, pv, AF.Exp,
                                    bias=c.cvec_sb[:, 0:1], scale=SCALE)
            w = [(c.s2PS, ps_cnt[i])]
            if i >= 4:
                w.append((c.s2AV, av_cnt[i - 4]))
            return f, w

        def mk_attnv(i):
            g, ig, hl, j, r = events[i]
            t0 = ig * 4
            st0 = max(0, j - t0)
            b = i % 4
            lhl = hl

            def f(t):
                for stt in range(st0, 4):
                    c1, c2 = a1col(stt), a2col(stt)
                    first = (j == 0) and stt in (0, 2)
                    vrhs = v_sb[:, j * 516 + lhl * 129:j * 516 + (lhl + 1) * 129]
                    t.matmul(accP[:, c1:c1 + 129],
                             p_sb[b][:, stt * 128:(stt + 1) * 128], vrhs,
                             start=first, stop=False, skip_group_check=True)
                    ins = t.matmul(accP[:, c2:c2 + 129],
                                   p_sb[b][:, 512 + stt * 128:512 + (stt + 1) * 128],
                                   vrhs, start=first, stop=False,
                                   skip_group_check=True)
                return ins
            w = [(c.s2EX, ex_cnt[i])]
            if j == 0 and r >= 1:
                w.append((c.s2FI, 2 * r))
            if i == first_ev_of_g[g]:
                w.append((c.sVL, 16 * (g + 1)))
            return f, w

        ACC_FREE = 2048

        def av(handle, offset, dims, free):
            return bass.AP(handle, offset, [[free, 128]] + [list(d) for d in dims])

        def mk_finA(r):
            g, ig, hl, ev0, ev1 = runs[r]

            def f1(v):
                if DEBUG_FIN:
                    stg = acc_st[r % 4]
                    for stt in range(4):
                        c1, c2 = a1col(stt), a2col(stt)
                        if DEBUG_FIN == "acc1":
                            ins = v.tensor_copy(stg[:, stt * 128:(stt + 1) * 128],
                                                accP[:, c1:c1 + 128])
                        elif DEBUG_FIN == "s1":
                            v.memset(stg[:, stt * 128:(stt + 1) * 128], 0.0)
                            v.tensor_copy(stg[:, stt * 128:stt * 128 + 1],
                                          accP[:, c1 + 128:c1 + 129])
                            ins = v.tensor_copy(stg[:, stt * 128 + 1:stt * 128 + 2],
                                                accP[:, c2 + 128:c2 + 129])
                    return ins
                sdim = [(512, 2), (130, 2), (1, 1)]
                fdim = [(2, 2), (1, 2), (1, 1)]
                ins = v.reciprocal(av(fr1, 0, fdim, 4), av(accP, 128, sdim, ACC_FREE))
                ins = v.reciprocal(av(fr2, 0, fdim, 4), av(accP, 1024 + 128, sdim, ACC_FREE))
                ssync(v, c.sVS, ins)
                ins = v.tensor_tensor(av(ft1, 0, fdim, 4),
                                      av(accP, 128, sdim, ACC_FREE),
                                      av(fr2, 0, fdim, 4), ALU.mult)
                cdim = [(512, 2), (130, 2), (1, 128)]
                odim = [(256, 2), (128, 2), (1, 128)]
                v.tensor_copy(av(fin_a1, 0, odim, 512), av(accP, 0, cdim, ACC_FREE))
                return v.tensor_copy(av(fin_a2, 0, odim, 512),
                                     av(accP, 1024, cdim, ACC_FREE))
            return f1, [(c.s2AV, av_cnt[ev1 - 1])]

        def mk_finB(r):
            g, ig, hl, ev0, ev1 = runs[r]
            stg = acc_st[r % 4]
            lamcol = g * 4 + hl

            def f1(v):
                if DEBUG_FIN:
                    return v.engine_nop()
                ins = v.tensor_scalar_mul(fkn[:, 0:4], ft1[:, 0:4],
                                          c.lamneg[:, lamcol:lamcol + 1])
                ssync(v, c.sVS, ins)
                for stt in range(4):
                    ins = v.scalar_tensor_tensor(fin_u[:, stt * 128:(stt + 1) * 128],
                                                 fin_a2[:, stt * 128:(stt + 1) * 128],
                                                 fkn[:, stt:stt + 1],
                                                 fin_a1[:, stt * 128:(stt + 1) * 128],
                                                 ALU.mult, ALU.add)
                ssync(v, c.sVS, ins)
                for stt in range(4):
                    ins = v.tensor_scalar_mul(stg[:, stt * 128:(stt + 1) * 128],
                                              fin_u[:, stt * 128:(stt + 1) * 128],
                                              fr1[:, stt:stt + 1])
                return ins
            w = [(c.s2FI, 2 + 2 * r)]
            if r >= 4:
                w.append((c.s2ASs[r % 4], 16 * (r // 4)))
            return f1, w

        def mk_accstore(r):
            g, ig, hl, ev0, ev1 = runs[r]
            stg = acc_st[r % 4]

            def f(gp, g=g):
                return gp.dma_start(
                    out=c.ACCd[g].ap()[ig * 512:(ig + 1) * 512,
                                       hl * 128:(hl + 1) * 128]
                        .rearrange("(s p) d -> p s d", p=128),
                    in_=stg.ap().rearrange("p (s d) -> p s d", s=4))

            return f, [(c.s2FB, r + 1)], r

        preps = []
        for sgrp in range(64):
            def fprep(g, sgrp=sgrp):
                g.dma_start(
                    out=c.wgB.ap()[sgrp],
                    in_=c.wgT.ap()[:, sgrp * 128:(sgrp + 1) * 128]
                        .rearrange("(cc p) h -> p cc h", p=128)
                ).then_inc(c.s4WP.h, 16)
                return g.dma_start(
                    out=c.wuB.ap()[sgrp],
                    in_=c.wuT.ap()[:, sgrp * 128:(sgrp + 1) * 128]
                        .rearrange("(cc p) h -> p cc h", p=128))
            preps.append((fprep, 32))
        for dcg in range(16):
            for hh in range(2):
                def fprepd(g, dcg=dcg, hh=hh):
                    return g.dma_start(
                        out=c.wdB.ap()[dcg][:, hh * 4096:(hh + 1) * 4096],
                        in_=c.wdT.ap()[hh * 4096:(hh + 1) * 4096,
                                       dcg * 128:(dcg + 1) * 128]
                            .rearrange("(sp p) d -> p sp d", p=128))
                preps.append((fprepd, 16))

        def emit_preps(k):
            while preps and k > 0:
                fn, inc = preps.pop(0)
                c.s4WP.n += inc - 16
                P.op("gpsimd", fn, incs=[(c.s4WP, 16)])
                k -= 1

        def emit_gather(g):
            need = 16 * (8 if g == 0 else 16)
            if SKIP_COLLECTIVE:
                def coll(gp, g=g):
                    gp.dma_start(out=c.ACCg[g].ap()[0:T, :],
                                 in_=c.ACCd[g].ap()).then_inc(c.sCC.h, 16)
                    return gp.dma_start(out=c.ACCg[g].ap()[T:2 * T, :],
                                        in_=c.ACCd[g].ap())
                c.sCC.n += 16
                P.op("gpsimd", coll, waits=[(sm, need) for sm in c.s2ASs],
                     incs=[(c.sCC, 16)])
            else:
                def coll(gp, g=g):
                    gp.collective_compute(
                        "AllGather", ALU.bypass,
                        replica_groups=[[0, 1], [2, 3], [4, 5], [6, 7]],
                        ins=[c.ACCd[g].ap().opt()], outs=[c.ACCg[g].ap().opt()]
                    ).then_inc(c.sCC.h)
                    return gp.nop()
                c.sCC.n += 1
                P.op("gpsimd", coll, waits=[(sm, need) for sm in c.s2ASs],
                     incs=[])
            cc_val = c.sCC.n

            def localize(sy, g=g):
                pid = sy.partition_id()
                with sy.register(f"rh{g}") as rh:
                    sy.reg_mod(rh, pid, 2)
                    for half in (0, 1):
                        with sy.If_eq(rh, half):
                            r0 = half * TOK
                            sy.dma_start(
                                out=c.ACCl.ap()[:, g * 512:(g + 1) * 512],
                                in_=c.ACCg[g].ap()[r0:r0 + TOK, :]
                            ).then_inc(c.sLC.h, 16)
                            sy.dma_start(
                                out=c.ACCl.ap()[:, 1024 + g * 512:
                                                1024 + (g + 1) * 512],
                                in_=c.ACCg[g].ap()[T + r0:T + r0 + TOK, :]
                            ).then_inc(c.sLC.h, 16)
                return sy.nop()
            if g == 0:
                c.sLC.n += 32
                P.op("sync", localize, waits=[(c.sCC, cc_val)], incs=[])
            else:
                c.loc1 = (localize, cc_val)

        c.s2L.n += 16
        P.op("sync", gloadKQ(0), incs=[(c.s2L, 16)])
        P.op("sync", gloadV(0), incs=[(c.sVL, 16)])
        if g_end_ev[1] > g_end_ev[0]:
            c.s2L.n += 16
            P.op("sync", gloadKQ(1), waits=[(c.s2PS, ps_cnt[g_end_ev[0] - 1])],
                 incs=[(c.s2L, 16)])
            P.op("sync", gloadV(1), waits=[(c.s2AV, av_cnt[g_end_ev[0] - 1])],
                 incs=[(c.sVL, 16)])

        def emit_attnv(i):
            g, ig, hl, j, r = events[i]
            fn, w = mk_attnv(i)
            P.op("tensor", fn, waits=w, incs=[(c.s2AV, 1)])
            if j == ig * 4 + 3:
                fn, w = mk_finA(r)
                P.op("vector", fn, waits=w, incs=[(c.s2FI, 2)])
                fn, w = mk_finB(r)
                P.op("vector", fn, waits=w, incs=[(c.s2FB, 1)])
                fn, w, rr = mk_accstore(r)
                P.op("gpsimd", fn, waits=w, incs=[(c.s2ASs[rr % 4], 16)])
                emit_preps(2)
                if r == NRUN // 2 - 1:
                    emit_gather(0)

        delayed = None
        for i in range(NEV):
            g, ig, hl, j, r = events[i]
            t0 = ig * 4
            if delayed is not None and events[delayed][0] != g:
                emit_attnv(delayed)
                delayed = None
            if j >= t0:
                fn, w = mk_mask(i)
                P.op("vector", fn, waits=w, incs=[(c.s2MK, 1)])
            fn, w = mk_scores(i)
            P.op("tensor", fn, waits=w, incs=[(c.s2PS, 1)])
            fn, w = mk_exp(i)
            P.op("scalar", fn, waits=w, incs=[(c.s2EX, 1)])
            if delayed is not None:
                emit_attnv(delayed)
            delayed = i
        if delayed is not None:
            emit_attnv(delayed)

        emit_preps(len(preps))
        P.barrier_inc(c.sBAR)
        emit_gather(1)
        P.emit()


def _p3(c, top):
    nc = c.nc
    with ExitStack() as st:
        E = st.enter_context
        wo_sb = E(nc.sbuf_tensor("wo_sb", [128, 16 * 2048], BF16))
        atm = [E(nc.sbuf_tensor(f"atm{i}", [128, 4 * 2048], BF16)) for i in range(2)]
        accT = E(nc.sbuf_tensor("accT", [128, 16 * 512], BF16))
        xres = [E(nc.sbuf_tensor(f"xres{i}", [128, 512], F32)) for i in range(2)]
        y1 = E(nc.sbuf_tensor("y1", [128, 16 * 512], F32))
        sq = [E(nc.sbuf_tensor(f"sq3{i}", [128, 512], F32)) for i in range(2)]
        rows = E(nc.sbuf_tensor("rows3", [1, 4 * 512], F32))
        tmp = [E(nc.sbuf_tensor(f"tmp3{i}", [128, 512], F32)) for i in range(2)]
        ps_t = [E(nc.psum_tensor(f"ps_t{i}", [128, 1024], BF16)) for i in range(2)]
        ps_o = [E(nc.psum_tensor(f"ps_o{i}", [128, 512], F32)) for i in range(2)]
        ps_sum = E(nc.psum_tensor("ps_sum", [128, 512], F32))
        ps_ssq = E(nc.psum_tensor("ps_ssq", [128, 512], F32))
        ps_mu = E(nc.psum_tensor("ps_mu", [128, 512], F32))
        ps_rs = E(nc.psum_tensor("ps_rs", [128, 512], F32))

        P = Phase(nc)
        P.barrier_wait(c.sBAR, 10)

        def woload(gp):
            return gp.dma_start(
                out=wo_sb.ap().rearrange("p (hc d) -> p hc d", hc=16),
                in_=c.wo.ap().rearrange("(hc p) d -> p hc d", p=128))
        P.op("gpsimd", woload, incs=[(c.s3WO, 16)])

        A_H = (0, 1, 2, 3, 8, 9, 10, 11)
        B_H = (4, 5, 6, 7, 12, 13, 14, 15)
        tp_i = 0
        for tb in range(4):
            def aloadA(sy, tb=tb):
                av = atm[tb % 2].ap().rearrange("p (ts h) -> p ts h", ts=4)
                sy.dma_start(
                    out=av[:, :, 0:512],
                    in_=c.ACCl.ap()[tb * 512:(tb + 1) * 512, 0:512]
                        .rearrange("(ts p) h -> p ts h", p=128)
                ).then_inc(c.s3Ls[tb % 2].h, 16)
                return sy.dma_start(
                    out=av[:, :, 1024:1536],
                    in_=c.ACCl.ap()[tb * 512:(tb + 1) * 512, 1024:1536]
                        .rearrange("(ts p) h -> p ts h", p=128))
            lw = [(c.sLC, 32)]
            if tb >= 2:
                lw.append((c.s3TP, 64 * (tb - 1)))
            c.s3Ls[tb % 2].n += 16
            P.op("sync", aloadA, waits=lw, incs=[(c.s3Ls[tb % 2], 16)])
            if tb == 0:
                fnloc, ccv = c.loc1
                c.sLC.n += 32
                P.op("sync", fnloc, waits=[(c.sCC, ccv)], incs=[])

            def aloadB(sy, tb=tb):
                av = atm[tb % 2].ap().rearrange("p (ts h) -> p ts h", ts=4)
                sy.dma_start(
                    out=av[:, :, 512:1024],
                    in_=c.ACCl.ap()[tb * 512:(tb + 1) * 512, 512:1024]
                        .rearrange("(ts p) h -> p ts h", p=128)
                ).then_inc(c.s3LBs[tb % 2].h, 16)
                return sy.dma_start(
                    out=av[:, :, 1536:2048],
                    in_=c.ACCl.ap()[tb * 512:(tb + 1) * 512, 1536:2048]
                        .rearrange("(ts p) h -> p ts h", p=128))
            lw = [(c.sLC, 64)]
            if tb >= 2:
                lw.append((c.s3TP, 64 * (tb - 1)))
            c.s3LBs[tb % 2].n += 16
            P.op("sync", aloadB, waits=lw, incs=[(c.s3LBs[tb % 2], 16)])

            for hcs, lsem in ((A_H, c.s3Ls[tb % 2]), (B_H, c.s3LBs[tb % 2])):
                for ts in range(4):
                    for hc in hcs:
                        def ftp(t, tb=tb, ts=ts, hc=hc, i=tp_i):
                            return t.transpose(
                                ps_t[i % 2][:, 0:128],
                                atm[tb % 2].ap().rearrange(
                                    "p (ts h) -> p ts h",
                                    ts=4)[:, ts, hc * 128:(hc + 1) * 128],
                                c.idt_bf[:, :])
                        w = [(lsem, 32 * (tb // 2 + 1)), (c.s3WO, 16)]
                        if tp_i >= 2:
                            w.append((c.s3DV, tp_i - 1))
                        P.op("tensor", ftp, waits=w, incs=[(c.s3TP, 1)])

                        def fev(v, ts=ts, hc=hc, i=tp_i):
                            return v.tensor_copy(
                                accT[:, hc * 512 + ts * 128:
                                     hc * 512 + ts * 128 + 128],
                                ps_t[i % 2][:, 0:128])
                        P.op("vector", fev, waits=[(c.s3TP, tp_i + 1)],
                             incs=[(c.s3DV, 1)])
                        tp_i += 1

            for dc in range(16):
                def fx(gp, tb=tb, dc=dc):
                    return gp.dma_start(
                        out=xres[dc % 2][:, :],
                        in_=c.xresT.ap()[dc * 128:(dc + 1) * 128,
                                         tb * 512:(tb + 1) * 512])
                w = []
                if tb * 16 + dc >= 2:
                    w.append((c.s3EO, tb * 16 + dc - 1))
                P.op("gpsimd", fx, waits=w, incs=[(c.s3XRs[dc % 2], 16)])

                def fwo(t, dc=dc, tb=tb):
                    for hc in range(16):
                        ins = t.matmul(
                            ps_o[dc % 2][:, :],
                            wo_sb[:, hc * 2048 + dc * 128:hc * 2048 + dc * 128 + 128],
                            accT[:, hc * 512:(hc + 1) * 512],
                            start=(hc == 0), stop=(hc == 15))
                    return ins
                w = [(c.s3DV, 64 * (tb + 1))]
                if tb * 16 + dc >= 2:
                    w.append((c.s3EO, tb * 16 + dc - 1))
                if tb >= 1:
                    pass
                P.op("tensor", fwo, waits=w, incs=[(c.s3MM, 1)])

                def fres(v, dc=dc):
                    return v.tensor_tensor(y1[:, dc * 512:(dc + 1) * 512],
                                           ps_o[dc % 2][:, :], xres[dc % 2][:, :],
                                           ALU.add)
                w = [(c.s3MM, c.s3MM.n),
                     (c.s3XRs[dc % 2], 16 * ((tb * 16 + dc) // 2 + 1))]
                if tb >= 1 and dc == 0:
                    w += [(c.s3SO, 16 * tb), (c.s3SB, 16 * tb)]
                P.op("vector", fres, waits=w, incs=[(c.s3EO, 1)])

                def fsq(v, dc=dc):
                    return v.tensor_tensor(sq[dc % 2][:, :],
                                           y1[:, dc * 512:(dc + 1) * 512],
                                           y1[:, dc * 512:(dc + 1) * 512], ALU.mult)
                w = [(c.s3EO, c.s3EO.n)]
                if tb * 16 + dc >= 2:
                    w.append((c.s3ST, 2 * (tb * 16 + dc - 1)))
                P.op("vector", fsq, waits=w, incs=[(c.s3SQ, 1)])

                def fst(t, dc=dc):
                    t.matmul(ps_sum[0:1, :], c.onesc[:, :],
                             y1[:, dc * 512:(dc + 1) * 512],
                             start=(dc == 0), stop=(dc == 15))
                    return t.matmul(ps_ssq[0:1, :], c.onesc[:, :], sq[dc % 2][:, :],
                                    start=(dc == 0), stop=(dc == 15))
                w = [(c.s3SQ, c.s3SQ.n)]
                if dc == 0 and tb >= 1:
                    w.append((c.s3SM, tb))
                P.op("tensor", fst, waits=w, incs=[(c.s3ST, 2)])

            def fsm(v, tb=tb):
                v.tensor_scalar_mul(rows[:, 0:512], ps_sum[0:1, :], 1.0 / D)
                ins = v.tensor_scalar_mul(rows[:, 512:1024], ps_ssq[0:1, :], 1.0 / D)
                ssync(v, c.sVS, ins)
                ins = v.tensor_tensor(rows[:, 1024:1536], rows[:, 0:512],
                                      rows[:, 0:512], ALU.mult)
                ssync(v, c.sVS, ins)
                return v.tensor_tensor(rows[:, 1024:1536], rows[:, 512:1024],
                                       rows[:, 1024:1536], ALU.subtract)
            P.op("vector", fsm, waits=[(c.s3ST, c.s3ST.n)], incs=[(c.s3SM, 1)])

            def fac(s, tb=tb):
                ins = s.activation(rows[:, 1536:2048], rows[:, 1024:1536], AF.Ln,
                                   bias=c.cvec_sb[0:1, 1:2])
                ssync(s, c.sSS, ins)
                return s.activation(rows[:, 1536:2048], rows[:, 1536:2048], AF.Exp,
                                    scale=-0.5)
            P.op("scalar", fac, waits=[(c.s3SM, c.s3SM.n)], incs=[(c.s3AC, 1)])

            def fbc(t, tb=tb):
                t.matmul(ps_mu[:, :], c.onesr[:, :], rows[:, 0:512],
                         start=True, stop=True)
                return t.matmul(ps_rs[:, :], c.onesr[:, :], rows[:, 1536:2048],
                                start=True, stop=True)
            w = [(c.s3AC, c.s3AC.n)]
            if tb >= 1:
                w.append((c.s3Y1, 32 * tb))
            P.op("tensor", fbc, waits=w, incs=[(c.s3MM, 1)])
            bc_done = c.s3MM.n

            for dcp in range(8):
                def fnm(v, dcp=dcp):
                    for k in (0, 1):
                        dc = 2 * dcp + k
                        ins = v.tensor_tensor(tmp[k][:, :],
                                              y1[:, dc * 512:(dc + 1) * 512],
                                              ps_mu[:, :], ALU.subtract)
                    ssync(v, c.sVS, ins)
                    for k in (0, 1):
                        dc = 2 * dcp + k
                        ins = v.tensor_tensor(y1[:, dc * 512:(dc + 1) * 512],
                                              tmp[k][:, :], ps_rs[:, :],
                                              ALU.mult)
                    return ins
                w = [(c.s3MM, bc_done)]
                if dcp >= 1:
                    w.append((c.s3Y1, c.s3Y1.n))
                P.op("vector", fnm, waits=w, incs=[(c.s3Y1, 4)])

            def fst1(sy, tb=tb):
                return sy.dma_start(
                    out=c.Y1f.ap()[:, tb * 512:(tb + 1) * 512]
                        .rearrange("(cc p) t -> p cc t", p=128),
                    in_=y1.ap().rearrange("p (cc t) -> p cc t", cc=16))
            P.op("sync", fst1, waits=[(c.s3Y1, c.s3Y1.n)], incs=[(c.s3SO, 16)])

            def fst2(gp, tb=tb):
                return gp.dma_start(
                    out=c.Y1b.ap()[:, tb * 512:(tb + 1) * 512]
                        .rearrange("(cc p) t -> p cc t", p=128),
                    in_=y1.ap().rearrange("p (cc t) -> p cc t", cc=16))
            P.op("gpsimd", fst2, waits=[(c.s3Y1, c.s3Y1.n)], incs=[(c.s3SB, 16)])

        P.wait("sync", c.s3SO, 16 * 4)
        P.wait("gpsimd", c.s3SB, 16 * 4)
        P.barrier_inc(c.sBAR)
        P.emit()


def _p4(c, top):
    nc = c.nc
    TP = 512
    NP = len(FFN_TP)
    with ExitStack() as st:
        E = st.enter_context
        y1b = E(nc.sbuf_tensor("y1b", [128, 16 * TP], BF16))
        gated = E(nc.sbuf_tensor("gated", [128, 64 * TP], BF16))
        wg_t = [E(nc.sbuf_tensor(f"wg{i}", [128, 2048], BF16)) for i in range(2)]
        wu_t = [E(nc.sbuf_tensor(f"wu{i}", [128, 2048], BF16)) for i in range(2)]
        wdc_t = [E(nc.sbuf_tensor(f"wdc{i}", [128, 8192], BF16)) for i in range(2)]
        silu = [E(nc.sbuf_tensor(f"silu{i}", [128, TP], F32)) for i in range(2)]
        res = [E(nc.sbuf_tensor(f"res4{i}", [128, TP], F32)) for i in range(2)]
        y2 = [E(nc.sbuf_tensor(f"y24{i}", [128, TP], F32)) for i in range(2)]
        sqb = [E(nc.sbuf_tensor(f"sq4{i}", [128, TP], F32)) for i in range(2)]
        nrm = [E(nc.sbuf_tensor(f"nrm4{i}", [128, TP], F32)) for i in range(2)]
        osb = [E(nc.sbuf_tensor(f"osb4{i}", [128, TP], F32)) for i in range(2)]
        tmpb = [E(nc.sbuf_tensor(f"tmp4{i}", [128, TP], F32)) for i in range(2)]
        rows = E(nc.sbuf_tensor("rows4", [1, 4 * 512], F32))
        ps_g = [E(nc.psum_tensor(f"ps_g{i}", [128, 512], F32)) for i in range(2)]
        ps_u = [E(nc.psum_tensor(f"ps_u{i}", [128, 512], F32)) for i in range(2)]
        ps_d = ps_g
        ps_sum = E(nc.psum_tensor("ps_sum4", [128, 512], F32))
        ps_ssq = E(nc.psum_tensor("ps_ssq4", [128, 512], F32))
        ps_mu = E(nc.psum_tensor("ps_mu4", [128, 512], F32))
        ps_rs = E(nc.psum_tensor("ps_rs4", [128, 512], F32))

        P = Phase(nc)
        P.barrier_wait(c.sBAR, 15)

        def emit_bc(np_):
            def fbc(t):
                t.matmul(ps_mu[:, :], c.onesr[:, :], rows[:, 0:512],
                         start=True, stop=True)
                return t.matmul(ps_rs[:, :], c.onesr[:, :], rows[:, 1536:2048],
                                start=True, stop=True)
            w = [(c.s5AC, np_ + 1)]
            if np_ >= 1:
                w.append((c.s5Y, 16 * np_))
            P.op("tensor", fbc, waits=w, incs=[(c.s5MM, 1)])

        def emit_norm(np_, dc):
            ni = np_ * 16 + dc
            t0 = np_ * TP

            def nload(gp):
                return gp.dma_start(
                    out=nrm[ni % 2][:, :],
                    in_=c.Y2f.ap()[dc * 128:(dc + 1) * 128, t0:t0 + TP])
            w = [(c.s4SOs[dc % 2], 16 * (np_ * 8 + dc // 2 + 1))]
            if ni >= 2:
                w.append((c.s5Y, ni - 1))
            P.op("gpsimd", nload, waits=w, incs=[(c.s4NLs[ni % 2], 16)])

            def vnorm(v):
                ins = v.tensor_tensor(tmpb[ni % 2][:, :], nrm[ni % 2][:, :],
                                      ps_mu[:, :], ALU.subtract)
                ssync(v, c.sVS, ins)
                return v.tensor_tensor(osb[ni % 2][:, :], tmpb[ni % 2][:, :],
                                       ps_rs[:, :], ALU.mult)
            w = [(c.s4NLs[ni % 2], 16 * (ni // 2 + 1)), (c.s5MM, np_ + 1)]
            if ni >= 2:
                w.append((c.s5SO, 16 * (ni - 1)))
            P.op("vector", vnorm, waits=w, incs=[(c.s5Y, 1)])

            def ostore(gp):
                return gp.dma_start(
                    out=c.out.ap()[dc * 128:(dc + 1) * 128, t0:t0 + TP],
                    in_=osb[ni % 2][:, :])
            P.op("gpsimd", ostore, waits=[(c.s5Y, ni + 1)],
                 incs=[(c.s5SO, 16)])

        tofs = 0
        for pi in range(NP):
            def yload(sy, tofs=tofs):
                return sy.dma_start(
                    out=y1b.ap().rearrange("p (cc t) -> p cc t", cc=16),
                    in_=c.Y1b.ap()[:, tofs:tofs + TP]
                        .rearrange("(cc p) t -> p cc t", p=128))
            w = []
            if pi >= 1:
                w.append((c.s4PE, c.s4PE.n))
            P.op("sync", yload, waits=w, incs=[(c.s4YL, 16)])

            if pi >= 1:
                emit_bc(pi - 1)
            for s in range(64):
                si = pi * 64 + s

                def wload(sy, s=s, si=si):
                    sy.dma_start(
                        out=wg_t[si % 2].ap(), in_=c.wgB.ap()[s]
                    ).then_inc(c.s4WLs[si % 2].h, 16)
                    return sy.dma_start(
                        out=wu_t[si % 2].ap(), in_=c.wuB.ap()[s])
                w = []
                if si < 2 and pi == 0:
                    w.append((c.s4WP, c.s4WP.n))
                if si >= 2:
                    w.append((c.s4PE, 2 * (si - 1)))
                c.s4WLs[si % 2].n += 16
                P.op("sync", wload, waits=w, incs=[(c.s4WLs[si % 2], 16)])

                def fgate(t, s=s, si=si):
                    for cc in range(16):
                        ins = t.matmul(ps_g[si % 2][:, :],
                                       wg_t[si % 2][:, cc * 128:(cc + 1) * 128],
                                       y1b[:, cc * TP:(cc + 1) * TP],
                                       start=(cc == 0), stop=(cc == 15))
                    return ins
                w = [(c.s4WLs[si % 2], 32 * (si // 2 + 1)),
                     (c.s4YL, 16 * (pi + 1))]
                if si >= 2:
                    w.append((c.s4DV, si - 1))
                if pi >= 1 and s < 2:
                    w.append((c.s4DV2, 16 * pi))
                P.op("tensor", fgate, waits=w, incs=[(c.s4PE, 1)])

                def fup(t, s=s, si=si):
                    for cc in range(16):
                        ins = t.matmul(ps_u[si % 2][:, :],
                                       wu_t[si % 2][:, cc * 128:(cc + 1) * 128],
                                       y1b[:, cc * TP:(cc + 1) * TP],
                                       start=(cc == 0), stop=(cc == 15))
                    return ins
                P.op("tensor", fup, incs=[(c.s4PE, 1)])

                def fsilu(s_, si=si):
                    return s_.activation(silu[si % 2][:, :],
                                         ps_g[si % 2][:, :], AF.Silu)
                P.op("scalar", fsilu, waits=[(c.s4PE, 2 * si + 1)],
                     incs=[(c.s4AC, 1)])

                def fmul(v, s=s, si=si):
                    return v.tensor_tensor(gated[:, s * TP:(s + 1) * TP],
                                           ps_u[si % 2][:, :],
                                           silu[si % 2][:, :], ALU.mult)
                w = [(c.s4PE, 2 * si + 2), (c.s4AC, c.s4AC.n)]
                if pi >= 1:
                    w.append((c.s4PE2, 16 * pi))
                P.op("vector", fmul, waits=w, incs=[(c.s4DV, 1)])

                if pi >= 1 and s % 4 == 1:
                    emit_norm(pi - 1, s // 4)

            gateup_done = c.s4DV.n
            stats_q = []
            for dc in range(16):
                di = pi * 16 + dc

                def wdload(sy, dc=dc, di=di):
                    return sy.dma_start(
                        out=wdc_t[di % 2].ap(), in_=c.wdB.ap()[dc])
                w = []
                if di >= 2:
                    w.append((c.s4PE2, di - 1))
                P.op("sync", wdload, waits=w, incs=[(c.s4WL2s[di % 2], 16)])

                def fdown(t, dc=dc, di=di):
                    for s in range(64):
                        ins = t.matmul(ps_d[di % 2][:, :],
                                       wdc_t[di % 2][:, s * 128:(s + 1) * 128],
                                       gated[:, s * TP:(s + 1) * TP],
                                       start=(s == 0), stop=(s == 63))
                    return ins
                w = [(c.s4WL2s[di % 2], 16 * (di // 2 + 1)),
                     (c.s4DV, gateup_done)]
                if di >= 2:
                    w.append((c.s4DV2, di - 1))
                P.op("tensor", fdown, waits=w, incs=[(c.s4PE2, 1)])
                while stats_q and stats_q[0][0] <= dc - 2:
                    _, fn, w2 = stats_q.pop(0)
                    P.op("tensor", fn, waits=w2, incs=[(c.s5ST, 2)])

                def fres(sy, dc=dc, di=di, tofs=tofs):
                    return sy.dma_start(
                        out=res[di % 2][:, :],
                        in_=c.Y1f.ap()[dc * 128:(dc + 1) * 128, tofs:tofs + TP])
                w = []
                if di >= 2:
                    w.append((c.s4DV2, di - 1))
                P.op("sync", fres, waits=w, incs=[(c.s4RLs[di % 2], 16)])

                def fev(v, dc=dc, di=di):
                    return v.tensor_tensor(y2[di % 2][:, :],
                                           ps_d[di % 2][:, :],
                                           res[di % 2][:, :], ALU.add)
                w = [(c.s4PE2, di + 1),
                     (c.s4RLs[di % 2], 16 * (di // 2 + 1))]
                if di >= 2:
                    w.append((c.s4SOs[di % 2], 16 * (di // 2)))
                if dc >= 2:
                    w.append((c.s5ST, 32 * pi + 2 * (dc - 1)))
                elif pi >= 1:
                    w.append((c.s5ST, 32 * pi))
                P.op("vector", fev, waits=w, incs=[(c.s4DV2, 1)])

                def fsq(v, dc=dc, di=di):
                    return v.tensor_tensor(sqb[dc % 2][:, :],
                                           y2[di % 2][:, :],
                                           y2[di % 2][:, :], ALU.mult)
                w = [(c.s4DV2, di + 1)]
                if dc >= 2:
                    w.append((c.s5ST, 32 * pi + 2 * (dc - 1)))
                P.op("vector", fsq, waits=w, incs=[(c.s5SQ, 1)])

                def fstat(t, dc=dc, di=di):
                    t.matmul(ps_sum[0:1, :], c.onesc[:, :], y2[di % 2][:, :],
                             start=(dc == 0), stop=(dc == 15))
                    return t.matmul(ps_ssq[0:1, :], c.onesc[:, :],
                                    sqb[dc % 2][:, :],
                                    start=(dc == 0), stop=(dc == 15))
                w2 = [(c.s5SQ, 16 * pi + dc + 1)]
                if dc == 0 and pi >= 1:
                    w2.append((c.s5SM, pi))
                stats_q.append((dc, fstat, w2))

                def fst(sy, dc=dc, di=di, tofs=tofs):
                    return sy.dma_start(
                        out=c.Y2f.ap()[dc * 128:(dc + 1) * 128, tofs:tofs + TP],
                        in_=y2[di % 2][:, :])
                P.op("sync", fst, waits=[(c.s4DV2, di + 1)],
                     incs=[(c.s4SOs[di % 2], 16)])
            while stats_q:
                _, fn, w2 = stats_q.pop(0)
                P.op("tensor", fn, waits=w2, incs=[(c.s5ST, 2)])

            def fsm(v, pi=pi):
                v.tensor_scalar_mul(rows[:, 0:512], ps_sum[0:1, :], 1.0 / D)
                ins = v.tensor_scalar_mul(rows[:, 512:1024], ps_ssq[0:1, :],
                                          1.0 / D)
                ssync(v, c.sVS, ins)
                ins = v.tensor_tensor(rows[:, 1024:1536], rows[:, 0:512],
                                      rows[:, 0:512], ALU.mult)
                ssync(v, c.sVS, ins)
                return v.tensor_tensor(rows[:, 1024:1536], rows[:, 512:1024],
                                       rows[:, 1024:1536], ALU.subtract)
            w = [(c.s5ST, 32 * (pi + 1))]
            if pi >= 1:
                w.append((c.s5MM, pi))
            P.op("vector", fsm, waits=w, incs=[(c.s5SM, 1)])

            def fac(s, pi=pi):
                ins = s.activation(rows[:, 1536:2048], rows[:, 1024:1536], AF.Ln,
                                   bias=c.cvec_sb[0:1, 1:2])
                ssync(s, c.sSS, ins)
                return s.activation(rows[:, 1536:2048], rows[:, 1536:2048],
                                    AF.Exp, scale=-0.5)
            P.op("scalar", fac, waits=[(c.s5SM, pi + 1)], incs=[(c.s5AC, 1)])
            tofs += TP

        emit_bc(NP - 1)
        for dc in range(16):
            emit_norm(NP - 1, dc)

        P.wait("gpsimd", c.s5SO, c.s5SO.n)
        P.wait("sync", c.s5SO, c.s5SO.n)
        P.barrier_inc(c.sBAR)
        P.emit()


def _host_inputs(inputs, debug_taps=()):
    x = np.asarray(inputs["L0_attn_x"], dtype=np.float32)
    Wq = np.asarray(inputs["L0_attn_Wq"], dtype=np.float32)
    Wk = np.asarray(inputs["L0_attn_Wk"], dtype=np.float32)
    Wv = np.asarray(inputs["L0_attn_Wv"], dtype=np.float32)
    Wo = np.asarray(inputs["L0_attn_Wo"], dtype=np.float32)
    lamb = np.asarray(inputs["L0_attn_lambda_init"], dtype=np.float32)
    wg = np.asarray(inputs["L0_ffn_w_gate"], dtype=np.float32)
    wu = np.asarray(inputs["L0_ffn_w_up"], dtype=np.float32)
    wd = np.asarray(inputs["L0_ffn_w_down"], dtype=np.float32)

    p = np.arange(128)[:, None]
    q = np.arange(128)[None, :]
    maskc = np.where(p > q, np.float32(NEG), np.float32(0.0)).astype(np.float32)
    ones_row = np.ones((1, 128), dtype=np.float32)
    ones_col = np.ones((128, 1), dtype=np.float32)
    idt = np.eye(128, dtype=np.float32)
    cvec = np.zeros((128, 2), dtype=np.float32)
    cvec[:, 0] = EBIAS
    cvec[:, 1] = LN_EPS
    wgT = np.ascontiguousarray(wg.T)
    wuT = np.ascontiguousarray(wu.T)
    wdT = np.ascontiguousarray(wd.T)
    Wo_c = np.ascontiguousarray(Wo)

    in_maps = []
    for core in range(8):
        b, half = core // 2, core % 2
        xT = np.ascontiguousarray(x[b].T)
        m = {
            "xT": xT,
            "wq": np.ascontiguousarray(Wq[:, half * 1024:(half + 1) * 1024]),
            "wk": np.ascontiguousarray(Wk[:, half * 1024:(half + 1) * 1024]),
            "wv": np.ascontiguousarray(Wv[:, half * 1024:(half + 1) * 1024]),
            "wo": Wo_c,
            "lam": np.ascontiguousarray(lamb[half * 8:(half + 1) * 8][None, :]),
            "wgT": wgT, "wuT": wuT, "wdT": wdT,
            "maskc": maskc, "ones_row": ones_row, "ones_col": ones_col,
            "idt": idt, "cvec": cvec,
            "xresT": np.ascontiguousarray(xT[:, half * TOK:(half + 1) * TOK]),
        }
        in_maps.append(m)
    return in_maps


def kernel(**inputs):
    nc = bass.Bass("TRN2", target_bir_lowering=False)
    build(nc, DEBUG_TAPS, PHASE_LIMIT)
    in_maps = _host_inputs(inputs, DEBUG_TAPS)
    res = run_bass_kernel_spmd(nc, in_maps, core_ids=list(range(8)))
    out = np.empty((B, T, D), dtype=np.float32)
    for core in range(8):
        b, half = core // 2, core % 2
        o = np.asarray(res.results[core]["out"], dtype=np.float32)
        out[b, half * TOK:(half + 1) * TOK, :] = o.T
    return out


# version 12
# speedup vs baseline: 1.0221x; 1.0221x over previous
import numpy as np
from contextlib import ExitStack

import concourse.bass as bass
import concourse.mybir as mybir
from concourse.bass_utils import run_bass_kernel_spmd

F32 = mybir.dt.float32
BF16 = mybir.dt.bfloat16
AF = mybir.ActivationFunctionType
ALU = mybir.AluOpType

B, T, D = 4, 4096, 2048
H, HD, HHD = 16, 128, 64
HL = 8
SCALE = 0.125
EBIAS = -40.0
HID = 8192
TOK = 2048
NB = T // 128
LN_EPS = 1e-5
NEG = -1.0e30
FFN_TP = [512, 512, 512, 512]
ENGINES = ("tensor", "scalar", "vector", "gpsimd", "sync")

DEBUG_TAPS = []
PHASE_LIMIT = 4
SKIP_COLLECTIVE = False
DEBUG_FIN = None
P2_TRUNC = None
P1_BLOCKS = 8


class Sem:
    def __init__(self, stack, nc, name):
        self.h = stack.enter_context(nc.semaphore(name))
        self.n = 0
        self.name = name


class Phase:

    def __init__(self, nc):
        self.nc = nc
        self.ops = []

    def op(self, eng, fn, waits=(), incs=()):
        waits = [(s, v) for (s, v) in waits]
        incs = [(s, k) for (s, k) in incs]
        for s, k in incs:
            s.n += k
        self.ops.append((eng, fn, waits, incs))

    def wait(self, eng, sem, val):
        self.ops.append((eng, None, [(sem, val)], []))

    def barrier_inc(self, sbar):
        for e in ENGINES:
            self.ops.append((e, "BARINC", [], [(sbar, 1)]))
        sbar.n += 0

    def barrier_wait(self, sbar, val):
        for e in ENGINES:
            self.ops.append((e, None, [(sbar, val)], []))

    def emit(self):
        with self.nc.Block() as block:
            def run(eng_name, eng):
                for e, fn, waits, incs in self.ops:
                    if e != eng_name:
                        continue
                    for s, v in waits:
                        eng.wait_ge(s.h, v)
                    if fn is None:
                        continue
                    if fn == "BARINC":
                        for s, k in incs:
                            eng.sem_inc(s.h, k)
                        continue
                    ins = fn(eng)
                    for s, k in incs:
                        ins.then_inc(s.h, k)

            @block.tensor
            def _(t):
                run("tensor", t)

            @block.scalar
            def _(sc):
                run("scalar", sc)

            @block.vector
            def _(v):
                run("vector", v)

            @block.gpsimd
            def _(gp):
                run("gpsimd", gp)

            @block.sync
            def _(sy):
                run("sync", sy)


def ssync(eng, sem, ins):
    sem.n += 1
    ins.then_inc(sem.h, 1)
    eng.wait_ge(sem.h, sem.n)


class Ctx:
    pass


def build(nc, debug_taps=(), phase_limit=5):
    c = Ctx()
    c.nc = nc
    dt = nc.dram_tensor
    c.xT = dt("xT", [D, T], F32, kind="ExternalInput")
    c.wq = dt("wq", [D, HL * HD], F32, kind="ExternalInput")
    c.wk = dt("wk", [D, HL * HD], F32, kind="ExternalInput")
    c.wv = dt("wv", [D, HL * HD], F32, kind="ExternalInput")
    c.wo = dt("wo", [H * HD, D], F32, kind="ExternalInput")
    c.lam = dt("lam", [1, HL], F32, kind="ExternalInput")
    c.wgT = dt("wgT", [D, HID], F32, kind="ExternalInput")
    c.wuT = dt("wuT", [D, HID], F32, kind="ExternalInput")
    c.wdT = dt("wdT", [HID, D], F32, kind="ExternalInput")
    c.maskc = dt("maskc", [128, 128], F32, kind="ExternalInput")
    c.ones_row = dt("ones_row", [1, 128], F32, kind="ExternalInput")
    c.ones_col = dt("ones_col", [128, 1], F32, kind="ExternalInput")
    c.idt = dt("idt", [128, 128], F32, kind="ExternalInput")
    c.xresT = dt("xresT", [D, TOK], F32, kind="ExternalInput")
    c.cvec = dt("cvec", [128, 2], F32, kind="ExternalInput")
    c.out = dt("out", [D, TOK], F32, kind="ExternalOutput")
    c.Kd = dt("Kd", [HL * HD, T], BF16)
    c.Qd = dt("Qd", [HL * HD, T], BF16)
    c.Vd = dt("Vd", [NB, 128, HL * 129], BF16)
    c.ACCd = [dt(f"ACCd{g}", [T, 4 * HD], BF16) for g in range(2)]
    c.ACCg = [dt(f"ACCg{g}", [2 * T, 4 * HD], BF16) for g in range(2)]
    c.ACCl = dt("ACCl", [TOK, H * HD], BF16)
    c.Y1f = dt("Y1f", [D, TOK], F32)
    c.Y1b = dt("Y1b", [D, TOK], BF16)
    c.Y2f = dt("Y2f", [D, TOK], F32)
    c.wgB = dt("wgB", [64, 128, 2048], BF16)
    c.wuB = dt("wuB", [64, 128, 2048], BF16)
    c.wdB = dt("wdB", [16, 128, 8192], BF16)

    with ExitStack() as top:
        E = top.enter_context
        c.lamneg = E(nc.sbuf_tensor("lamneg", [128, HL], F32))
        c.idt_bf = E(nc.sbuf_tensor("idt_bf", [128, 128], BF16))
        c.mask_sb = E(nc.sbuf_tensor("mask_sb", [128, 128], F32))
        c.onesr = E(nc.sbuf_tensor("onesr", [1, 128], F32))
        c.onesc = E(nc.sbuf_tensor("onesc", [128, 1], F32))
        c.cvec_sb = E(nc.sbuf_tensor("cvec_sb", [128, 2], F32))
        for nm in ("sBAR", "sW", "sX", "sPE", "sEVK", "sEVV", "sSTK", "sSTV",
                   "sMISC", "s2L", "s2PS", "s2MK", "s2EX", "s2AV", "s2FI",
                   "s2AS", "sCC", "s3L", "s3XR", "s3WO", "s3TP", "s3DV",
                   "s3EO", "s3SQ", "s3ST", "s3SM", "s3AC", "s3MM", "s3Y1",
                   "s3SO", "s3SB", "s4YL", "s4WL", "s4PE", "s4AC", "s4DV",
                   "s4WL2", "s4PE2", "s4RL", "s4DV2", "s4SO", "s5L", "sVL", "s5SQ",
                   "s5ST", "s5SM", "s5AC", "s5MM", "s5Y", "s5SO", "sVS",
                   "sSS", "sLC", "s2FB", "s4WP"):
            setattr(c, nm, Sem(top, nc, nm))
        c.sXs = [Sem(top, nc, f"sXs{i}") for i in range(2)]
        c.sSTKs = [Sem(top, nc, f"sSTKs{i}") for i in range(4)]
        c.sSTVs = [Sem(top, nc, f"sSTVs{i}") for i in range(2)]
        c.s2ASs = [Sem(top, nc, f"s2ASs{i}") for i in range(4)]
        c.s3Ls = [Sem(top, nc, f"s3Ls{i}") for i in range(2)]
        c.s3LBs = [Sem(top, nc, f"s3LBs{i}") for i in range(2)]
        c.s3XRs = [Sem(top, nc, f"s3XRs{i}") for i in range(2)]
        c.s4WLs = [Sem(top, nc, f"s4WLs{i}") for i in range(2)]
        c.s4WL2s = [Sem(top, nc, f"s4WL2s{i}") for i in range(2)]
        c.s4RLs = [Sem(top, nc, f"s4RLs{i}") for i in range(2)]
        c.s4SOs = [Sem(top, nc, f"s4SOs{i}") for i in range(2)]
        c.s4NLs = [Sem(top, nc, f"s4NLs{i}") for i in range(2)]

        phases = [_p1, _p2, _p3, _p4]
        for i in range(min(phase_limit, 4)):
            phases[i](c, top)
        _taps(c, phase_limit, debug_taps)
    return nc


def _taps(c, nphases, debug_taps):
    if not debug_taps:
        return
    nc = c.nc
    P = Phase(nc)
    P.barrier_wait(c.sBAR, 5 * nphases)
    sT = c.s5SO
    for nm in debug_taps:
        src = getattr(c, nm)

        def ftap(gp, src=src, nm=nm):
            tout = nc.dram_tensor(f"tap_{nm}", list(src.ap().shape),
                                  F32, kind="ExternalOutput")
            return gp.dma_start(out=tout.ap(), in_=src.ap())
        P.op("gpsimd", ftap, incs=[(sT, 16)])
    P.wait("gpsimd", sT, sT.n)
    P.emit()


def _p1(c, top):
    nc = c.nc
    with ExitStack() as st:
        E = st.enter_context
        wq_sb = E(nc.sbuf_tensor("wq_sb", [128, 16 * 1024], BF16))
        wk_sb = E(nc.sbuf_tensor("wk_sb", [128, 16 * 1024], BF16))
        wv_sb = E(nc.sbuf_tensor("wv_sb", [128, 16 * 1024], BF16))
        xb = [E(nc.sbuf_tensor(f"xb{i}", [128, 16 * 512], BF16)) for i in range(2)]
        kq_st = [E(nc.sbuf_tensor(f"kqst{i}", [128, 512], BF16)) for i in range(4)]
        v_st = [E(nc.sbuf_tensor(f"vst{i}", [128, 4 * 1032], BF16)) for i in range(2)]
        lam_sb = E(nc.sbuf_tensor("lam_sb", [1, HL], F32))
        lam_t1 = E(nc.sbuf_tensor("lam_t1", [1, HL], F32))
        lam_t2 = E(nc.sbuf_tensor("lam_t2", [1, HL], F32))
        psp = [E(nc.psum_tensor(f"psp{i}", [128, 512], F32)) for i in range(4)]
        ps_lam = E(nc.psum_tensor("ps_lam", [128, HL], F32))

        P = Phase(nc)
        W_SB = {"q": wq_sb, "k": wk_sb, "v": wv_sb}
        KD = {"q": c.Qd, "k": c.Kd}

        def wload(dst, src):
            return lambda g: g.dma_start(
                out=dst.ap().rearrange("p (c h) -> p c h", c=16),
                in_=src.ap().rearrange("(c p) h -> p c h", p=128))
        P.op("gpsimd", wload(wq_sb, c.wq), incs=[(c.sW, 16)])
        P.op("gpsimd", wload(wk_sb, c.wk), incs=[(c.sW, 16)])
        P.op("gpsimd", wload(wv_sb, c.wv), incs=[(c.sW, 16)])
        P.op("gpsimd", lambda g: g.dma_start(out=c.idt_bf[:, :], in_=c.idt[:, :]),
             incs=[(c.sW, 16)])
        P.op("gpsimd", lambda g: g.dma_start(out=c.mask_sb[:, :], in_=c.maskc[:, :]),
             incs=[(c.sW, 16)])
        P.op("gpsimd", lambda g: g.dma_start(out=c.onesr[:, :], in_=c.ones_row[:, :]),
             incs=[(c.sW, 16)])
        P.op("gpsimd", lambda g: g.dma_start(out=c.onesc[:, :], in_=c.ones_col[:, :]),
             incs=[(c.sW, 16)])
        P.op("gpsimd", lambda g: g.dma_start(out=lam_sb[:, :], in_=c.lam[:, :]),
             incs=[(c.sW, 16)])
        P.op("gpsimd", lambda g: g.dma_start(out=c.cvec_sb[:, :], in_=c.cvec[:, :]),
             incs=[(c.sW, 16)])
        WTOT = c.sW.n

        P.op("vector", lambda v, s=0: v.memset(v_st[s].ap(), 1.0), incs=[(c.sMISC, 1)])
        P.op("vector", lambda v, s=1: v.memset(v_st[s].ap(), 1.0), incs=[(c.sMISC, 1)])
        VONES = c.sMISC.n

        P.op("scalar", lambda s: s.activation(lam_t1[:, :], lam_sb[:, :], AF.Exp,
                                              scale=-1.0),
             waits=[(c.sW, WTOT)], incs=[(c.sMISC, 1)])
        P.op("vector", lambda v: v.tensor_scalar_add(lam_t2[:, :], lam_t1[:, :], 1.0),
             waits=[(c.sMISC, VONES + 1)], incs=[(c.sMISC, 1)])
        P.op("vector", lambda v: v.reciprocal(lam_t1[:, :], lam_t2[:, :]),
             waits=[(c.sMISC, c.sMISC.n)], incs=[(c.sMISC, 1)])
        P.op("vector", lambda v: v.tensor_scalar_mul(lam_t2[:, :], lam_t1[:, :], -1.0),
             waits=[(c.sMISC, c.sMISC.n)], incs=[(c.sMISC, 1)])
        LAMR = c.sMISC.n
        P.op("tensor", lambda t: t.matmul(ps_lam[:, :], c.onesr[:, :], lam_t2[:, :],
                                          start=True, stop=True),
             waits=[(c.sMISC, LAMR)], incs=[(c.sMISC, 1)])
        P.op("vector", lambda v: v.tensor_copy(c.lamneg[:, :], ps_lam[:, :]),
             waits=[(c.sMISC, c.sMISC.n)], incs=[(c.sMISC, 1)])

        uses = ([("k", hl) for hl in range(HL)] + [("q", hl) for hl in range(HL)]
                + [("v", j2, hc) for j2 in range(4) for hc in range(2)])
        NU = len(uses)
        evac_of = {}
        kq_i = 0
        for bi in range(P1_BLOCKS):
            def xload(g, bi=bi):
                return g.dma_start(
                    out=xb[bi % 2].ap().rearrange("p (cc t) -> p cc t", cc=16),
                    in_=c.xT.ap()[:, bi * 512:(bi + 1) * 512]
                        .rearrange("(cc p) t -> p cc t", p=128))
            xw = []
            if bi >= 2:
                xw.append((c.sPE, NU * (bi - 1)))
            P.op("gpsimd", xload, waits=xw, incs=[(c.sXs[bi % 2], 16)])

            for ui, use in enumerate(uses):
                idx = bi * NU + ui
                ps = psp[idx % 4]
                mmw = [(c.sXs[bi % 2], 16 * (bi // 2 + 1))]
                if bi == 0 and ui == 0:
                    mmw.append((c.sW, WTOT))
                if idx >= 4:
                    mmw.append(evac_of[idx - 4])

                if use[0] in ("k", "q"):
                    kind, hl = use

                    def mm(t, ps=ps, kind=kind, hl=hl, x=xb[bi % 2]):
                        w = W_SB[kind]
                        for cc in range(16):
                            ins = t.matmul(
                                ps[:, :],
                                w[:, cc * 1024 + hl * 128:cc * 1024 + hl * 128 + 128],
                                x[:, cc * 512:(cc + 1) * 512],
                                start=(cc == 0), stop=(cc == 15))
                        return ins
                    P.op("tensor", mm, waits=mmw, incs=[(c.sPE, 1)])
                    stg = kq_st[kq_i % 4]
                    evw = [(c.sPE, idx + 1)]
                    if kq_i >= 4:
                        evw.append((c.sSTKs[kq_i % 4], 16 * (kq_i // 4)))

                    def ev(s, stg=stg, ps=ps):
                        return s.copy(stg[:, :], ps[:, :])
                    P.op("scalar", ev, waits=evw, incs=[(c.sEVK, 1)])
                    evac_of[idx] = (c.sEVK, c.sEVK.n)
                    dram = KD[kind]

                    def stf(sy, dram=dram, hl=hl, bi=bi, stg=stg):
                        return sy.dma_start(
                            out=dram.ap()[hl * 128:(hl + 1) * 128,
                                          bi * 512:(bi + 1) * 512],
                            in_=stg[:, :])
                    P.op("sync", stf, waits=[(c.sEVK, c.sEVK.n)],
                         incs=[(c.sSTKs[kq_i % 4], 16)])
                    kq_i += 1
                else:
                    _, j2, hc = use

                    def mm(t, ps=ps, j2=j2, hc=hc, x=xb[bi % 2]):
                        for cc in range(16):
                            ins = t.matmul(
                                ps[:, :],
                                x[:, cc * 512 + j2 * 128:cc * 512 + j2 * 128 + 128],
                                wv_sb[:, cc * 1024 + hc * 512:cc * 1024 + (hc + 1) * 512],
                                start=(cc == 0), stop=(cc == 15))
                        return ins
                    P.op("tensor", mm, waits=mmw, incs=[(c.sPE, 1)])
                    stg = v_st[bi % 2]
                    evw = [(c.sPE, idx + 1)]
                    if bi >= 2 and ui == 16:
                        evw.append((c.sSTVs[bi % 2], 16 * (bi // 2)))

                    def ev(v, stg=stg, ps=ps, j2=j2, hc=hc):
                        dst = stg.ap().rearrange("p (j h d) -> p j h d", j=4, h=HL)
                        dst = dst[:, j2, hc * 4:(hc + 1) * 4, 0:128]
                        return v.tensor_copy(
                            dst, ps.ap().rearrange("p (h d) -> p h d", h=4))
                    P.op("vector", ev, waits=evw, incs=[(c.sEVV, 1)])
                    evac_of[idx] = (c.sEVV, c.sEVV.n)

            def vstf(sy, bi=bi, stg=v_st[bi % 2]):
                return sy.dma_start(
                    out=c.Vd.ap()[4 * bi:4 * bi + 4, :, :]
                        .rearrange("j p h -> p j h"),
                    in_=stg.ap().rearrange("p (j h) -> p j h", j=4))
            P.op("sync", vstf, waits=[(c.sEVV, c.sEVV.n)],
                 incs=[(c.sSTVs[bi % 2], 16)])

        for sm in c.sSTKs + c.sSTVs:
            P.wait("sync", sm, sm.n)
        P.barrier_inc(c.sBAR)
        P.emit()


def _p2(c, top):
    nc = c.nc
    with ExitStack() as st:
        E = st.enter_context
        k_sb = E(nc.sbuf_tensor("k_sb", [128, 4 * T], BF16))
        q_sb = E(nc.sbuf_tensor("q_sb", [128, 4 * T], BF16))
        v_sb = E(nc.sbuf_tensor("v_sb", [128, NB * 4 * 129], BF16))
        p_sb = [E(nc.sbuf_tensor(f"p_sb{i}", [128, 1024], BF16)) for i in range(4)]
        acc_st = [E(nc.sbuf_tensor(f"accst{i}", [128, 512], BF16)) for i in range(4)]
        fin_a1 = E(nc.sbuf_tensor("fin_a1", [128, 4 * 128], F32))
        fin_a2 = E(nc.sbuf_tensor("fin_a2", [128, 4 * 128], F32))
        fin_u = E(nc.sbuf_tensor("fin_u", [128, 4 * 128], F32))
        fr1 = E(nc.sbuf_tensor("fr1", [128, 4], F32))
        fr2 = E(nc.sbuf_tensor("fr2", [128, 4], F32))
        ft1 = E(nc.sbuf_tensor("ft1", [128, 4], F32))
        fkn = E(nc.sbuf_tensor("fkn", [128, 4], F32))
        psS = [E(nc.psum_tensor(f"psS{i}", [128, 1024], F32)) for i in range(2)]
        accP = E(nc.psum_tensor("accP", [128, 2048], F32))

        def a1col(stt):
            return (stt // 2) * 512 + (stt % 2) * 130

        def a2col(stt):
            return 1024 + a1col(stt)

        P = Phase(nc)
        P.barrier_wait(c.sBAR, 5)

        events = []
        runs = []
        for g in range(2):
            for ig in range(8):
                for hl in range(4):
                    if P2_TRUNC is not None and len(runs) >= P2_TRUNC:
                        continue
                    r = len(runs)
                    t0 = ig * 4
                    ev0 = len(events)
                    for j in range(t0 + 4):
                        events.append((g, ig, hl, j, r))
                    runs.append((g, ig, hl, ev0, len(events)))
        NEV = len(events)
        NRUN = len(runs)

        av_after_group = {}
        n = 0
        for g in range(2):
            for _, ig, hl, j, r in events:
                pass
        av_cnt = []
        tot = 0
        for (g, ig, hl, j, r) in events:
            tot += 1
            av_cnt.append(tot)
        g_end_ev = {0: 0, 1: 0}
        first_ev_of_g = {}
        for i, (g, ig, hl, j, r) in enumerate(events):
            g_end_ev[g] = i + 1
            if g not in first_ev_of_g:
                first_ev_of_g[g] = i
        ps_cnt = list(range(1, NEV + 1))
        ex_cnt = list(range(1, NEV + 1))
        mk_cnt = []
        mtot = 0
        for (g, ig, hl, j, r) in events:
            if j >= ig * 4:
                mtot += 1
            mk_cnt.append(mtot)

        def gloadKQ(g):
            def f(sy, g=g):
                sy.dma_start(
                    out=k_sb.ap().rearrange("p (h t) -> p h t", h=4),
                    in_=c.Kd.ap()[g * 512:(g + 1) * 512, :]
                        .rearrange("(h p) t -> p h t", p=128)
                ).then_inc(c.s2L.h, 16)
                return sy.dma_start(
                    out=q_sb.ap().rearrange("p (h t) -> p h t", h=4),
                    in_=c.Qd.ap()[g * 512:(g + 1) * 512, :]
                        .rearrange("(h p) t -> p h t", p=128))
            return f

        def gloadV(g):
            def f(sy, g=g):
                return sy.dma_start(
                    out=v_sb.ap().rearrange("p (b h) -> p b h", b=NB),
                    in_=c.Vd.ap()[:, :, g * 516:(g + 1) * 516]
                        .rearrange("b p h -> p b h"))
            return f

        def mk_scores(i):
            g, ig, hl, j, r = events[i]
            t0 = ig * 4
            diag = j >= t0
            co = 128 * max(0, j - t0)
            b = i % 2

            def f(t):
                t.matmul(psS[b][:, co:512],
                         k_sb[0:64, hl * T + j * 128:hl * T + j * 128 + 128],
                         q_sb[0:64, hl * T + ig * 512 + co:hl * T + (ig + 1) * 512],
                         start=not diag, stop=True, tile_position=(0, 0),
                         skip_group_check=diag)
                return t.matmul(
                    psS[b][:, 512 + co:1024],
                    k_sb[64:128, hl * T + j * 128:hl * T + j * 128 + 128],
                    q_sb[64:128, hl * T + ig * 512 + co:hl * T + (ig + 1) * 512],
                    start=not diag, stop=True, tile_position=(64, 0),
                    skip_group_check=diag)
            w = []
            if i >= 2:
                w.append((c.s2EX, i - 1))
            w.append((c.s2L, 32 * (g + 1)))
            if j >= t0:
                w.append((c.s2MK, mk_cnt[i]))
            return f, w

        def mk_mask(i):
            g, ig, hl, j, r = events[i]
            t0 = ig * 4
            sd = j - t0
            co = 128 * sd
            b = i % 2

            def f(v):
                ins = v.tensor_copy(psS[b][:, co:co + 128], c.mask_sb[:, :])
                ins = v.tensor_copy(psS[b][:, 512 + co:512 + co + 128],
                                    c.mask_sb[:, :])
                if co + 128 < 512:
                    ins = v.memset(psS[b][:, co + 128:512], 0.0)
                    ins = v.memset(psS[b][:, 512 + co + 128:1024], 0.0)
                return ins
            w = []
            if i >= 2:
                w.append((c.s2EX, i - 1))
            return f, w

        def mk_exp(i):
            b = i % 2
            g, ig, hl, j, r = events[i]
            t0 = ig * 4
            co = 128 * max(0, j - t0)

            def f(s):
                pv = psS[b].ap().rearrange("p (two n) -> p two n", two=2)[:, :, co:512]
                ov = p_sb[i % 4].ap().rearrange("p (two n) -> p two n",
                                                two=2)[:, :, co:512]
                return s.activation(ov, pv, AF.Exp,
                                    bias=c.cvec_sb[:, 0:1], scale=SCALE)
            w = [(c.s2PS, ps_cnt[i])]
            if i >= 4:
                w.append((c.s2AV, av_cnt[i - 4]))
            return f, w

        def mk_attnv(i):
            g, ig, hl, j, r = events[i]
            t0 = ig * 4
            st0 = max(0, j - t0)
            b = i % 4
            lhl = hl

            def f(t):
                for stt in range(st0, 4):
                    c1, c2 = a1col(stt), a2col(stt)
                    first = (j == 0) and stt in (0, 2)
                    vrhs = v_sb[:, j * 516 + lhl * 129:j * 516 + (lhl + 1) * 129]
                    t.matmul(accP[:, c1:c1 + 129],
                             p_sb[b][:, stt * 128:(stt + 1) * 128], vrhs,
                             start=first, stop=False, skip_group_check=True)
                    ins = t.matmul(accP[:, c2:c2 + 129],
                                   p_sb[b][:, 512 + stt * 128:512 + (stt + 1) * 128],
                                   vrhs, start=first, stop=False,
                                   skip_group_check=True)
                return ins
            w = [(c.s2EX, ex_cnt[i])]
            if j == 0 and r >= 1:
                w.append((c.s2FI, 2 * r))
            if i == first_ev_of_g[g]:
                w.append((c.sVL, 16 * (g + 1)))
            return f, w

        ACC_FREE = 2048

        def av(handle, offset, dims, free):
            return bass.AP(handle, offset, [[free, 128]] + [list(d) for d in dims])

        def mk_finA(r):
            g, ig, hl, ev0, ev1 = runs[r]

            def f1(v):
                if DEBUG_FIN:
                    stg = acc_st[r % 4]
                    for stt in range(4):
                        c1, c2 = a1col(stt), a2col(stt)
                        if DEBUG_FIN == "acc1":
                            ins = v.tensor_copy(stg[:, stt * 128:(stt + 1) * 128],
                                                accP[:, c1:c1 + 128])
                        elif DEBUG_FIN == "s1":
                            v.memset(stg[:, stt * 128:(stt + 1) * 128], 0.0)
                            v.tensor_copy(stg[:, stt * 128:stt * 128 + 1],
                                          accP[:, c1 + 128:c1 + 129])
                            ins = v.tensor_copy(stg[:, stt * 128 + 1:stt * 128 + 2],
                                                accP[:, c2 + 128:c2 + 129])
                    return ins
                sdim = [(512, 2), (130, 2), (1, 1)]
                fdim = [(2, 2), (1, 2), (1, 1)]
                ins = v.reciprocal(av(fr1, 0, fdim, 4), av(accP, 128, sdim, ACC_FREE))
                ins = v.reciprocal(av(fr2, 0, fdim, 4), av(accP, 1024 + 128, sdim, ACC_FREE))
                ssync(v, c.sVS, ins)
                ins = v.tensor_tensor(av(ft1, 0, fdim, 4),
                                      av(accP, 128, sdim, ACC_FREE),
                                      av(fr2, 0, fdim, 4), ALU.mult)
                cdim = [(512, 2), (130, 2), (1, 128)]
                odim = [(256, 2), (128, 2), (1, 128)]
                v.tensor_copy(av(fin_a1, 0, odim, 512), av(accP, 0, cdim, ACC_FREE))
                return v.tensor_copy(av(fin_a2, 0, odim, 512),
                                     av(accP, 1024, cdim, ACC_FREE))
            return f1, [(c.s2AV, av_cnt[ev1 - 1])]

        def mk_finB(r):
            g, ig, hl, ev0, ev1 = runs[r]
            stg = acc_st[r % 4]
            lamcol = g * 4 + hl

            def f1(v):
                if DEBUG_FIN:
                    return v.engine_nop()
                ins = v.tensor_scalar_mul(fkn[:, 0:4], ft1[:, 0:4],
                                          c.lamneg[:, lamcol:lamcol + 1])
                ssync(v, c.sVS, ins)
                for stt in range(4):
                    ins = v.scalar_tensor_tensor(fin_u[:, stt * 128:(stt + 1) * 128],
                                                 fin_a2[:, stt * 128:(stt + 1) * 128],
                                                 fkn[:, stt:stt + 1],
                                                 fin_a1[:, stt * 128:(stt + 1) * 128],
                                                 ALU.mult, ALU.add)
                ssync(v, c.sVS, ins)
                for stt in range(4):
                    ins = v.tensor_scalar_mul(stg[:, stt * 128:(stt + 1) * 128],
                                              fin_u[:, stt * 128:(stt + 1) * 128],
                                              fr1[:, stt:stt + 1])
                return ins
            w = [(c.s2FI, 2 + 2 * r)]
            if r >= 4:
                w.append((c.s2ASs[r % 4], 16 * (r // 4)))
            return f1, w

        def mk_accstore(r):
            g, ig, hl, ev0, ev1 = runs[r]
            stg = acc_st[r % 4]

            def f(gp, g=g):
                return gp.dma_start(
                    out=c.ACCd[g].ap()[ig * 512:(ig + 1) * 512,
                                       hl * 128:(hl + 1) * 128]
                        .rearrange("(s p) d -> p s d", p=128),
                    in_=stg.ap().rearrange("p (s d) -> p s d", s=4))

            return f, [(c.s2FB, r + 1)], r

        preps = []
        for sgrp in range(64):
            def fprep(g, sgrp=sgrp):
                g.dma_start(
                    out=c.wgB.ap()[sgrp],
                    in_=c.wgT.ap()[:, sgrp * 128:(sgrp + 1) * 128]
                        .rearrange("(cc p) h -> p cc h", p=128)
                ).then_inc(c.s4WP.h, 16)
                return g.dma_start(
                    out=c.wuB.ap()[sgrp],
                    in_=c.wuT.ap()[:, sgrp * 128:(sgrp + 1) * 128]
                        .rearrange("(cc p) h -> p cc h", p=128))
            preps.append((fprep, 32))
        for dcg in range(16):
            for hh in range(2):
                def fprepd(g, dcg=dcg, hh=hh):
                    return g.dma_start(
                        out=c.wdB.ap()[dcg][:, hh * 4096:(hh + 1) * 4096],
                        in_=c.wdT.ap()[hh * 4096:(hh + 1) * 4096,
                                       dcg * 128:(dcg + 1) * 128]
                            .rearrange("(sp p) d -> p sp d", p=128))
                preps.append((fprepd, 16))

        def emit_preps(k):
            while preps and k > 0:
                fn, inc = preps.pop(0)
                c.s4WP.n += inc - 16
                P.op("gpsimd", fn, incs=[(c.s4WP, 16)])
                k -= 1

        def emit_gather(g):
            need = 16 * (8 if g == 0 else 16)
            if SKIP_COLLECTIVE:
                def coll(gp, g=g):
                    gp.dma_start(out=c.ACCg[g].ap()[0:T, :],
                                 in_=c.ACCd[g].ap()).then_inc(c.sCC.h, 16)
                    return gp.dma_start(out=c.ACCg[g].ap()[T:2 * T, :],
                                        in_=c.ACCd[g].ap())
                c.sCC.n += 16
                P.op("gpsimd", coll, waits=[(sm, need) for sm in c.s2ASs],
                     incs=[(c.sCC, 16)])
            else:
                def coll(gp, g=g):
                    gp.collective_compute(
                        "AllGather", ALU.bypass,
                        replica_groups=[[0, 1], [2, 3], [4, 5], [6, 7]],
                        ins=[c.ACCd[g].ap().opt()], outs=[c.ACCg[g].ap().opt()]
                    ).then_inc(c.sCC.h)
                    return gp.nop()
                c.sCC.n += 1
                P.op("gpsimd", coll, waits=[(sm, need) for sm in c.s2ASs],
                     incs=[])
            cc_val = c.sCC.n

            def localize(sy, g=g):
                pid = sy.partition_id()
                with sy.register(f"rh{g}") as rh:
                    sy.reg_mod(rh, pid, 2)
                    for half in (0, 1):
                        with sy.If_eq(rh, half):
                            r0 = half * TOK
                            sy.dma_start(
                                out=c.ACCl.ap()[:, g * 512:(g + 1) * 512],
                                in_=c.ACCg[g].ap()[r0:r0 + TOK, :]
                            ).then_inc(c.sLC.h, 16)
                            sy.dma_start(
                                out=c.ACCl.ap()[:, 1024 + g * 512:
                                                1024 + (g + 1) * 512],
                                in_=c.ACCg[g].ap()[T + r0:T + r0 + TOK, :]
                            ).then_inc(c.sLC.h, 16)
                return sy.nop()
            if g == 0:
                c.sLC.n += 32
                P.op("sync", localize, waits=[(c.sCC, cc_val)], incs=[])
            else:
                c.loc1 = (localize, cc_val)

        c.s2L.n += 16
        P.op("sync", gloadKQ(0), incs=[(c.s2L, 16)])
        P.op("sync", gloadV(0), incs=[(c.sVL, 16)])
        if g_end_ev[1] > g_end_ev[0]:
            c.s2L.n += 16
            P.op("sync", gloadKQ(1), waits=[(c.s2PS, ps_cnt[g_end_ev[0] - 1])],
                 incs=[(c.s2L, 16)])
            P.op("sync", gloadV(1), waits=[(c.s2AV, av_cnt[g_end_ev[0] - 1])],
                 incs=[(c.sVL, 16)])

        def emit_attnv(i):
            g, ig, hl, j, r = events[i]
            fn, w = mk_attnv(i)
            P.op("tensor", fn, waits=w, incs=[(c.s2AV, 1)])
            if j == ig * 4 + 3:
                fn, w = mk_finA(r)
                P.op("vector", fn, waits=w, incs=[(c.s2FI, 2)])
                fn, w = mk_finB(r)
                P.op("vector", fn, waits=w, incs=[(c.s2FB, 1)])
                fn, w, rr = mk_accstore(r)
                P.op("gpsimd", fn, waits=w, incs=[(c.s2ASs[rr % 4], 16)])
                emit_preps(2)
                if r == NRUN // 2 - 1:
                    emit_gather(0)

        delayed = None
        for i in range(NEV):
            g, ig, hl, j, r = events[i]
            t0 = ig * 4
            if delayed is not None and events[delayed][0] != g:
                emit_attnv(delayed)
                delayed = None
            if j >= t0:
                fn, w = mk_mask(i)
                P.op("vector", fn, waits=w, incs=[(c.s2MK, 1)])
            fn, w = mk_scores(i)
            P.op("tensor", fn, waits=w, incs=[(c.s2PS, 1)])
            fn, w = mk_exp(i)
            P.op("scalar", fn, waits=w, incs=[(c.s2EX, 1)])
            if delayed is not None:
                emit_attnv(delayed)
            delayed = i
        if delayed is not None:
            emit_attnv(delayed)

        emit_preps(len(preps))
        P.barrier_inc(c.sBAR)
        emit_gather(1)
        P.emit()


def _p3(c, top):
    nc = c.nc
    with ExitStack() as st:
        E = st.enter_context
        wo_sb = E(nc.sbuf_tensor("wo_sb", [128, 16 * 2048], BF16))
        atm = [E(nc.sbuf_tensor(f"atm{i}", [128, 4 * 2048], BF16)) for i in range(2)]
        accT = E(nc.sbuf_tensor("accT", [128, 16 * 512], BF16))
        xres = [E(nc.sbuf_tensor(f"xres{i}", [128, 512], F32)) for i in range(2)]
        y1 = E(nc.sbuf_tensor("y1", [128, 16 * 512], F32))
        sq = [E(nc.sbuf_tensor(f"sq3{i}", [128, 512], F32)) for i in range(2)]
        rows = E(nc.sbuf_tensor("rows3", [1, 4 * 512], F32))
        tmp = [E(nc.sbuf_tensor(f"tmp3{i}", [128, 512], F32)) for i in range(2)]
        ps_t = [E(nc.psum_tensor(f"ps_t{i}", [128, 1024], BF16)) for i in range(2)]
        ps_o = [E(nc.psum_tensor(f"ps_o{i}", [128, 512], F32)) for i in range(2)]
        ps_sum = E(nc.psum_tensor("ps_sum", [128, 512], F32))
        ps_ssq = E(nc.psum_tensor("ps_ssq", [128, 512], F32))
        ps_mu = E(nc.psum_tensor("ps_mu", [128, 512], F32))
        ps_rs = E(nc.psum_tensor("ps_rs", [128, 512], F32))

        P = Phase(nc)
        P.barrier_wait(c.sBAR, 10)

        def woload(gp):
            return gp.dma_start(
                out=wo_sb.ap().rearrange("p (hc d) -> p hc d", hc=16),
                in_=c.wo.ap().rearrange("(hc p) d -> p hc d", p=128))
        P.op("gpsimd", woload, incs=[(c.s3WO, 16)])

        A_H = (0, 1, 2, 3, 8, 9, 10, 11)
        B_H = (4, 5, 6, 7, 12, 13, 14, 15)
        tp_i = 0
        for tb in range(4):
            def aloadA(sy, tb=tb):
                av = atm[tb % 2].ap().rearrange("p (ts h) -> p ts h", ts=4)
                sy.dma_start(
                    out=av[:, :, 0:512],
                    in_=c.ACCl.ap()[tb * 512:(tb + 1) * 512, 0:512]
                        .rearrange("(ts p) h -> p ts h", p=128)
                ).then_inc(c.s3Ls[tb % 2].h, 16)
                return sy.dma_start(
                    out=av[:, :, 1024:1536],
                    in_=c.ACCl.ap()[tb * 512:(tb + 1) * 512, 1024:1536]
                        .rearrange("(ts p) h -> p ts h", p=128))
            lw = [(c.sLC, 32)]
            if tb >= 2:
                lw.append((c.s3TP, 64 * (tb - 1)))
            c.s3Ls[tb % 2].n += 16
            P.op("sync", aloadA, waits=lw, incs=[(c.s3Ls[tb % 2], 16)])
            if tb == 0:
                fnloc, ccv = c.loc1
                c.sLC.n += 32
                P.op("sync", fnloc, waits=[(c.sCC, ccv)], incs=[])

            def aloadB(sy, tb=tb):
                av = atm[tb % 2].ap().rearrange("p (ts h) -> p ts h", ts=4)
                sy.dma_start(
                    out=av[:, :, 512:1024],
                    in_=c.ACCl.ap()[tb * 512:(tb + 1) * 512, 512:1024]
                        .rearrange("(ts p) h -> p ts h", p=128)
                ).then_inc(c.s3LBs[tb % 2].h, 16)
                return sy.dma_start(
                    out=av[:, :, 1536:2048],
                    in_=c.ACCl.ap()[tb * 512:(tb + 1) * 512, 1536:2048]
                        .rearrange("(ts p) h -> p ts h", p=128))
            lw = [(c.sLC, 64)]
            if tb >= 2:
                lw.append((c.s3TP, 64 * (tb - 1)))
            c.s3LBs[tb % 2].n += 16
            P.op("sync", aloadB, waits=lw, incs=[(c.s3LBs[tb % 2], 16)])

            for hcs, lsem in ((A_H, c.s3Ls[tb % 2]), (B_H, c.s3LBs[tb % 2])):
                for ts in range(4):
                    for hc in hcs:
                        def ftp(t, tb=tb, ts=ts, hc=hc, i=tp_i):
                            return t.transpose(
                                ps_t[i % 2][:, 0:128],
                                atm[tb % 2].ap().rearrange(
                                    "p (ts h) -> p ts h",
                                    ts=4)[:, ts, hc * 128:(hc + 1) * 128],
                                c.idt_bf[:, :])
                        w = [(lsem, 32 * (tb // 2 + 1)), (c.s3WO, 16)]
                        if tp_i >= 2:
                            w.append((c.s3DV, tp_i - 1))
                        P.op("tensor", ftp, waits=w, incs=[(c.s3TP, 1)])

                        def fev(s_, ts=ts, hc=hc, i=tp_i):
                            return s_.copy(
                                accT[:, hc * 512 + ts * 128:
                                     hc * 512 + ts * 128 + 128],
                                ps_t[i % 2][:, 0:128])
                        P.op("scalar", fev, waits=[(c.s3TP, tp_i + 1)],
                             incs=[(c.s3DV, 1)])
                        tp_i += 1

            for dc in range(16):
                def fx(gp, tb=tb, dc=dc):
                    return gp.dma_start(
                        out=xres[dc % 2][:, :],
                        in_=c.xresT.ap()[dc * 128:(dc + 1) * 128,
                                         tb * 512:(tb + 1) * 512])
                w = []
                if tb * 16 + dc >= 2:
                    w.append((c.s3EO, tb * 16 + dc - 1))
                P.op("gpsimd", fx, waits=w, incs=[(c.s3XRs[dc % 2], 16)])

                def fwo(t, dc=dc, tb=tb):
                    for hc in range(16):
                        ins = t.matmul(
                            ps_o[dc % 2][:, :],
                            wo_sb[:, hc * 2048 + dc * 128:hc * 2048 + dc * 128 + 128],
                            accT[:, hc * 512:(hc + 1) * 512],
                            start=(hc == 0), stop=(hc == 15))
                    return ins
                w = [(c.s3DV, 64 * (tb + 1))]
                if tb * 16 + dc >= 2:
                    w.append((c.s3EO, tb * 16 + dc - 1))
                if tb >= 1:
                    pass
                P.op("tensor", fwo, waits=w, incs=[(c.s3MM, 1)])

                def fres(v, dc=dc):
                    return v.tensor_tensor(y1[:, dc * 512:(dc + 1) * 512],
                                           ps_o[dc % 2][:, :], xres[dc % 2][:, :],
                                           ALU.add)
                w = [(c.s3MM, c.s3MM.n),
                     (c.s3XRs[dc % 2], 16 * ((tb * 16 + dc) // 2 + 1))]
                if tb >= 1 and dc == 0:
                    w += [(c.s3SO, 16 * tb), (c.s3SB, 16 * tb)]
                P.op("vector", fres, waits=w, incs=[(c.s3EO, 1)])

                def fsq(v, dc=dc):
                    return v.tensor_tensor(sq[dc % 2][:, :],
                                           y1[:, dc * 512:(dc + 1) * 512],
                                           y1[:, dc * 512:(dc + 1) * 512], ALU.mult)
                w = [(c.s3EO, c.s3EO.n)]
                if tb * 16 + dc >= 2:
                    w.append((c.s3ST, 2 * (tb * 16 + dc - 1)))
                P.op("vector", fsq, waits=w, incs=[(c.s3SQ, 1)])

                def fst(t, dc=dc):
                    t.matmul(ps_sum[0:1, :], c.onesc[:, :],
                             y1[:, dc * 512:(dc + 1) * 512],
                             start=(dc == 0), stop=(dc == 15))
                    return t.matmul(ps_ssq[0:1, :], c.onesc[:, :], sq[dc % 2][:, :],
                                    start=(dc == 0), stop=(dc == 15))
                w = [(c.s3SQ, c.s3SQ.n)]
                if dc == 0 and tb >= 1:
                    w.append((c.s3SM, tb))
                P.op("tensor", fst, waits=w, incs=[(c.s3ST, 2)])

            def fsm(v, tb=tb):
                v.tensor_scalar_mul(rows[:, 0:512], ps_sum[0:1, :], 1.0 / D)
                ins = v.tensor_scalar_mul(rows[:, 512:1024], ps_ssq[0:1, :], 1.0 / D)
                ssync(v, c.sVS, ins)
                ins = v.tensor_tensor(rows[:, 1024:1536], rows[:, 0:512],
                                      rows[:, 0:512], ALU.mult)
                ssync(v, c.sVS, ins)
                return v.tensor_tensor(rows[:, 1024:1536], rows[:, 512:1024],
                                       rows[:, 1024:1536], ALU.subtract)
            P.op("vector", fsm, waits=[(c.s3ST, c.s3ST.n)], incs=[(c.s3SM, 1)])

            def fac(s, tb=tb):
                ins = s.activation(rows[:, 1536:2048], rows[:, 1024:1536], AF.Ln,
                                   bias=c.cvec_sb[0:1, 1:2])
                ssync(s, c.sSS, ins)
                return s.activation(rows[:, 1536:2048], rows[:, 1536:2048], AF.Exp,
                                    scale=-0.5)
            P.op("scalar", fac, waits=[(c.s3SM, c.s3SM.n)], incs=[(c.s3AC, 1)])

            def fbc(t, tb=tb):
                t.matmul(ps_mu[:, :], c.onesr[:, :], rows[:, 0:512],
                         start=True, stop=True)
                return t.matmul(ps_rs[:, :], c.onesr[:, :], rows[:, 1536:2048],
                                start=True, stop=True)
            w = [(c.s3AC, c.s3AC.n)]
            if tb >= 1:
                w.append((c.s3Y1, 32 * tb))
            P.op("tensor", fbc, waits=w, incs=[(c.s3MM, 1)])
            bc_done = c.s3MM.n

            for dcp in range(8):
                def fnm(v, dcp=dcp):
                    for k in (0, 1):
                        dc = 2 * dcp + k
                        ins = v.tensor_tensor(tmp[k][:, :],
                                              y1[:, dc * 512:(dc + 1) * 512],
                                              ps_mu[:, :], ALU.subtract)
                    ssync(v, c.sVS, ins)
                    for k in (0, 1):
                        dc = 2 * dcp + k
                        ins = v.tensor_tensor(y1[:, dc * 512:(dc + 1) * 512],
                                              tmp[k][:, :], ps_rs[:, :],
                                              ALU.mult)
                    return ins
                w = [(c.s3MM, bc_done)]
                if dcp >= 1:
                    w.append((c.s3Y1, c.s3Y1.n))
                P.op("vector", fnm, waits=w, incs=[(c.s3Y1, 4)])

            def fst1(sy, tb=tb):
                return sy.dma_start(
                    out=c.Y1f.ap()[:, tb * 512:(tb + 1) * 512]
                        .rearrange("(cc p) t -> p cc t", p=128),
                    in_=y1.ap().rearrange("p (cc t) -> p cc t", cc=16))
            P.op("sync", fst1, waits=[(c.s3Y1, c.s3Y1.n)], incs=[(c.s3SO, 16)])

            def fst2(gp, tb=tb):
                return gp.dma_start(
                    out=c.Y1b.ap()[:, tb * 512:(tb + 1) * 512]
                        .rearrange("(cc p) t -> p cc t", p=128),
                    in_=y1.ap().rearrange("p (cc t) -> p cc t", cc=16))
            P.op("gpsimd", fst2, waits=[(c.s3Y1, c.s3Y1.n)], incs=[(c.s3SB, 16)])

        P.wait("sync", c.s3SO, 16 * 4)
        P.wait("gpsimd", c.s3SB, 16 * 4)
        P.barrier_inc(c.sBAR)
        P.emit()


def _p4(c, top):
    nc = c.nc
    TP = 512
    NP = len(FFN_TP)
    with ExitStack() as st:
        E = st.enter_context
        y1b = E(nc.sbuf_tensor("y1b", [128, 16 * TP], BF16))
        gated = E(nc.sbuf_tensor("gated", [128, 64 * TP], BF16))
        wg_t = [E(nc.sbuf_tensor(f"wg{i}", [128, 2048], BF16)) for i in range(2)]
        wu_t = [E(nc.sbuf_tensor(f"wu{i}", [128, 2048], BF16)) for i in range(2)]
        wdc_t = [E(nc.sbuf_tensor(f"wdc{i}", [128, 8192], BF16)) for i in range(2)]
        silu = [E(nc.sbuf_tensor(f"silu{i}", [128, TP], F32)) for i in range(2)]
        res = [E(nc.sbuf_tensor(f"res4{i}", [128, TP], F32)) for i in range(2)]
        y2 = [E(nc.sbuf_tensor(f"y24{i}", [128, TP], F32)) for i in range(2)]
        sqb = [E(nc.sbuf_tensor(f"sq4{i}", [128, TP], F32)) for i in range(2)]
        nrm = [E(nc.sbuf_tensor(f"nrm4{i}", [128, TP], F32)) for i in range(2)]
        osb = [E(nc.sbuf_tensor(f"osb4{i}", [128, TP], F32)) for i in range(2)]
        tmpb = [E(nc.sbuf_tensor(f"tmp4{i}", [128, TP], F32)) for i in range(2)]
        rows = E(nc.sbuf_tensor("rows4", [1, 4 * 512], F32))
        ps_g = [E(nc.psum_tensor(f"ps_g{i}", [128, 512], F32)) for i in range(2)]
        ps_u = [E(nc.psum_tensor(f"ps_u{i}", [128, 512], F32)) for i in range(2)]
        ps_d = ps_g
        ps_sum = E(nc.psum_tensor("ps_sum4", [128, 512], F32))
        ps_ssq = E(nc.psum_tensor("ps_ssq4", [128, 512], F32))
        ps_mu = E(nc.psum_tensor("ps_mu4", [128, 512], F32))
        ps_rs = E(nc.psum_tensor("ps_rs4", [128, 512], F32))

        P = Phase(nc)
        P.barrier_wait(c.sBAR, 15)

        def emit_bc(np_):
            def fbc(t):
                t.matmul(ps_mu[:, :], c.onesr[:, :], rows[:, 0:512],
                         start=True, stop=True)
                return t.matmul(ps_rs[:, :], c.onesr[:, :], rows[:, 1536:2048],
                                start=True, stop=True)
            w = [(c.s5AC, np_ + 1)]
            if np_ >= 1:
                w.append((c.s5Y, 16 * np_))
            P.op("tensor", fbc, waits=w, incs=[(c.s5MM, 1)])

        def emit_norm(np_, dc):
            ni = np_ * 16 + dc
            t0 = np_ * TP

            def nload(gp):
                return gp.dma_start(
                    out=nrm[ni % 2][:, :],
                    in_=c.Y2f.ap()[dc * 128:(dc + 1) * 128, t0:t0 + TP])
            w = [(c.s4SOs[dc % 2], 16 * (np_ * 8 + dc // 2 + 1))]
            if ni >= 2:
                w.append((c.s5Y, ni - 1))
            P.op("gpsimd", nload, waits=w, incs=[(c.s4NLs[ni % 2], 16)])

            def vnorm(v):
                ins = v.tensor_tensor(tmpb[ni % 2][:, :], nrm[ni % 2][:, :],
                                      ps_mu[:, :], ALU.subtract)
                ssync(v, c.sVS, ins)
                return v.tensor_tensor(osb[ni % 2][:, :], tmpb[ni % 2][:, :],
                                       ps_rs[:, :], ALU.mult)
            w = [(c.s4NLs[ni % 2], 16 * (ni // 2 + 1)), (c.s5MM, np_ + 1)]
            if ni >= 2:
                w.append((c.s5SO, 16 * (ni - 1)))
            P.op("vector", vnorm, waits=w, incs=[(c.s5Y, 1)])

            def ostore(gp):
                return gp.dma_start(
                    out=c.out.ap()[dc * 128:(dc + 1) * 128, t0:t0 + TP],
                    in_=osb[ni % 2][:, :])
            P.op("gpsimd", ostore, waits=[(c.s5Y, ni + 1)],
                 incs=[(c.s5SO, 16)])

        tofs = 0
        for pi in range(NP):
            def yload(sy, tofs=tofs):
                return sy.dma_start(
                    out=y1b.ap().rearrange("p (cc t) -> p cc t", cc=16),
                    in_=c.Y1b.ap()[:, tofs:tofs + TP]
                        .rearrange("(cc p) t -> p cc t", p=128))
            w = []
            if pi >= 1:
                w.append((c.s4PE, c.s4PE.n))
            P.op("sync", yload, waits=w, incs=[(c.s4YL, 16)])

            if pi >= 1:
                emit_bc(pi - 1)
            for s in range(64):
                si = pi * 64 + s

                def wload(sy, s=s, si=si):
                    sy.dma_start(
                        out=wg_t[si % 2].ap(), in_=c.wgB.ap()[s]
                    ).then_inc(c.s4WLs[si % 2].h, 16)
                    return sy.dma_start(
                        out=wu_t[si % 2].ap(), in_=c.wuB.ap()[s])
                w = []
                if si < 2 and pi == 0:
                    w.append((c.s4WP, c.s4WP.n))
                if si >= 2:
                    w.append((c.s4PE, 2 * (si - 1)))
                c.s4WLs[si % 2].n += 16
                P.op("sync", wload, waits=w, incs=[(c.s4WLs[si % 2], 16)])

                def fgate(t, s=s, si=si):
                    for cc in range(16):
                        ins = t.matmul(ps_g[si % 2][:, :],
                                       wg_t[si % 2][:, cc * 128:(cc + 1) * 128],
                                       y1b[:, cc * TP:(cc + 1) * TP],
                                       start=(cc == 0), stop=(cc == 15))
                    return ins
                w = [(c.s4WLs[si % 2], 32 * (si // 2 + 1)),
                     (c.s4YL, 16 * (pi + 1))]
                if si >= 2:
                    w.append((c.s4DV, si - 1))
                if pi >= 1 and s < 2:
                    w.append((c.s4DV2, 16 * pi))
                P.op("tensor", fgate, waits=w, incs=[(c.s4PE, 1)])

                def fup(t, s=s, si=si):
                    for cc in range(16):
                        ins = t.matmul(ps_u[si % 2][:, :],
                                       wu_t[si % 2][:, cc * 128:(cc + 1) * 128],
                                       y1b[:, cc * TP:(cc + 1) * TP],
                                       start=(cc == 0), stop=(cc == 15))
                    return ins
                P.op("tensor", fup, incs=[(c.s4PE, 1)])

                def fsilu(s_, si=si):
                    return s_.activation(silu[si % 2][:, :],
                                         ps_g[si % 2][:, :], AF.Silu)
                P.op("scalar", fsilu, waits=[(c.s4PE, 2 * si + 1)],
                     incs=[(c.s4AC, 1)])

                def fmul(v, s=s, si=si):
                    return v.tensor_tensor(gated[:, s * TP:(s + 1) * TP],
                                           ps_u[si % 2][:, :],
                                           silu[si % 2][:, :], ALU.mult)
                w = [(c.s4PE, 2 * si + 2), (c.s4AC, c.s4AC.n)]
                if pi >= 1:
                    w.append((c.s4PE2, 16 * pi))
                P.op("vector", fmul, waits=w, incs=[(c.s4DV, 1)])

                if pi >= 1 and s % 4 == 1:
                    emit_norm(pi - 1, s // 4)

            gateup_done = c.s4DV.n
            stats_q = []
            for dc in range(16):
                di = pi * 16 + dc

                def wdload(sy, dc=dc, di=di):
                    return sy.dma_start(
                        out=wdc_t[di % 2].ap(), in_=c.wdB.ap()[dc])
                w = []
                if di >= 2:
                    w.append((c.s4PE2, di - 1))
                P.op("sync", wdload, waits=w, incs=[(c.s4WL2s[di % 2], 16)])

                def fdown(t, dc=dc, di=di):
                    for s in range(64):
                        ins = t.matmul(ps_d[di % 2][:, :],
                                       wdc_t[di % 2][:, s * 128:(s + 1) * 128],
                                       gated[:, s * TP:(s + 1) * TP],
                                       start=(s == 0), stop=(s == 63))
                    return ins
                w = [(c.s4WL2s[di % 2], 16 * (di // 2 + 1)),
                     (c.s4DV, gateup_done)]
                if di >= 2:
                    w.append((c.s4DV2, di - 1))
                P.op("tensor", fdown, waits=w, incs=[(c.s4PE2, 1)])
                while stats_q and stats_q[0][0] <= dc - 2:
                    _, fn, w2 = stats_q.pop(0)
                    P.op("tensor", fn, waits=w2, incs=[(c.s5ST, 2)])

                def fres(sy, dc=dc, di=di, tofs=tofs):
                    return sy.dma_start(
                        out=res[di % 2][:, :],
                        in_=c.Y1f.ap()[dc * 128:(dc + 1) * 128, tofs:tofs + TP])
                w = []
                if di >= 2:
                    w.append((c.s4DV2, di - 1))
                P.op("sync", fres, waits=w, incs=[(c.s4RLs[di % 2], 16)])

                def fev(v, dc=dc, di=di):
                    return v.tensor_tensor(y2[di % 2][:, :],
                                           ps_d[di % 2][:, :],
                                           res[di % 2][:, :], ALU.add)
                w = [(c.s4PE2, di + 1),
                     (c.s4RLs[di % 2], 16 * (di // 2 + 1))]
                if di >= 2:
                    w.append((c.s4SOs[di % 2], 16 * (di // 2)))
                if dc >= 2:
                    w.append((c.s5ST, 32 * pi + 2 * (dc - 1)))
                elif pi >= 1:
                    w.append((c.s5ST, 32 * pi))
                P.op("vector", fev, waits=w, incs=[(c.s4DV2, 1)])

                def fsq(v, dc=dc, di=di):
                    return v.tensor_tensor(sqb[dc % 2][:, :],
                                           y2[di % 2][:, :],
                                           y2[di % 2][:, :], ALU.mult)
                w = [(c.s4DV2, di + 1)]
                if dc >= 2:
                    w.append((c.s5ST, 32 * pi + 2 * (dc - 1)))
                P.op("vector", fsq, waits=w, incs=[(c.s5SQ, 1)])

                def fstat(t, dc=dc, di=di):
                    t.matmul(ps_sum[0:1, :], c.onesc[:, :], y2[di % 2][:, :],
                             start=(dc == 0), stop=(dc == 15))
                    return t.matmul(ps_ssq[0:1, :], c.onesc[:, :],
                                    sqb[dc % 2][:, :],
                                    start=(dc == 0), stop=(dc == 15))
                w2 = [(c.s5SQ, 16 * pi + dc + 1)]
                if dc == 0 and pi >= 1:
                    w2.append((c.s5SM, pi))
                stats_q.append((dc, fstat, w2))

                def fst(sy, dc=dc, di=di, tofs=tofs):
                    return sy.dma_start(
                        out=c.Y2f.ap()[dc * 128:(dc + 1) * 128, tofs:tofs + TP],
                        in_=y2[di % 2][:, :])
                P.op("sync", fst, waits=[(c.s4DV2, di + 1)],
                     incs=[(c.s4SOs[di % 2], 16)])
            while stats_q:
                _, fn, w2 = stats_q.pop(0)
                P.op("tensor", fn, waits=w2, incs=[(c.s5ST, 2)])

            def fsm(v, pi=pi):
                v.tensor_scalar_mul(rows[:, 0:512], ps_sum[0:1, :], 1.0 / D)
                ins = v.tensor_scalar_mul(rows[:, 512:1024], ps_ssq[0:1, :],
                                          1.0 / D)
                ssync(v, c.sVS, ins)
                ins = v.tensor_tensor(rows[:, 1024:1536], rows[:, 0:512],
                                      rows[:, 0:512], ALU.mult)
                ssync(v, c.sVS, ins)
                return v.tensor_tensor(rows[:, 1024:1536], rows[:, 512:1024],
                                       rows[:, 1024:1536], ALU.subtract)
            w = [(c.s5ST, 32 * (pi + 1))]
            if pi >= 1:
                w.append((c.s5MM, pi))
            P.op("vector", fsm, waits=w, incs=[(c.s5SM, 1)])

            def fac(s, pi=pi):
                ins = s.activation(rows[:, 1536:2048], rows[:, 1024:1536], AF.Ln,
                                   bias=c.cvec_sb[0:1, 1:2])
                ssync(s, c.sSS, ins)
                return s.activation(rows[:, 1536:2048], rows[:, 1536:2048],
                                    AF.Exp, scale=-0.5)
            P.op("scalar", fac, waits=[(c.s5SM, pi + 1)], incs=[(c.s5AC, 1)])
            tofs += TP

        emit_bc(NP - 1)
        for dc in range(16):
            emit_norm(NP - 1, dc)

        P.wait("gpsimd", c.s5SO, c.s5SO.n)
        P.wait("sync", c.s5SO, c.s5SO.n)
        P.barrier_inc(c.sBAR)
        P.emit()


def _host_inputs(inputs, debug_taps=()):
    x = np.asarray(inputs["L0_attn_x"], dtype=np.float32)
    Wq = np.asarray(inputs["L0_attn_Wq"], dtype=np.float32)
    Wk = np.asarray(inputs["L0_attn_Wk"], dtype=np.float32)
    Wv = np.asarray(inputs["L0_attn_Wv"], dtype=np.float32)
    Wo = np.asarray(inputs["L0_attn_Wo"], dtype=np.float32)
    lamb = np.asarray(inputs["L0_attn_lambda_init"], dtype=np.float32)
    wg = np.asarray(inputs["L0_ffn_w_gate"], dtype=np.float32)
    wu = np.asarray(inputs["L0_ffn_w_up"], dtype=np.float32)
    wd = np.asarray(inputs["L0_ffn_w_down"], dtype=np.float32)

    p = np.arange(128)[:, None]
    q = np.arange(128)[None, :]
    maskc = np.where(p > q, np.float32(NEG), np.float32(0.0)).astype(np.float32)
    ones_row = np.ones((1, 128), dtype=np.float32)
    ones_col = np.ones((128, 1), dtype=np.float32)
    idt = np.eye(128, dtype=np.float32)
    cvec = np.zeros((128, 2), dtype=np.float32)
    cvec[:, 0] = EBIAS
    cvec[:, 1] = LN_EPS
    wgT = np.ascontiguousarray(wg.T)
    wuT = np.ascontiguousarray(wu.T)
    wdT = np.ascontiguousarray(wd.T)
    Wo_c = np.ascontiguousarray(Wo)

    in_maps = []
    for core in range(8):
        b, half = core // 2, core % 2
        xT = np.ascontiguousarray(x[b].T)
        m = {
            "xT": xT,
            "wq": np.ascontiguousarray(Wq[:, half * 1024:(half + 1) * 1024]),
            "wk": np.ascontiguousarray(Wk[:, half * 1024:(half + 1) * 1024]),
            "wv": np.ascontiguousarray(Wv[:, half * 1024:(half + 1) * 1024]),
            "wo": Wo_c,
            "lam": np.ascontiguousarray(lamb[half * 8:(half + 1) * 8][None, :]),
            "wgT": wgT, "wuT": wuT, "wdT": wdT,
            "maskc": maskc, "ones_row": ones_row, "ones_col": ones_col,
            "idt": idt, "cvec": cvec,
            "xresT": np.ascontiguousarray(xT[:, half * TOK:(half + 1) * TOK]),
        }
        in_maps.append(m)
    return in_maps


def kernel(**inputs):
    nc = bass.Bass("TRN2", target_bir_lowering=False)
    build(nc, DEBUG_TAPS, PHASE_LIMIT)
    in_maps = _host_inputs(inputs, DEBUG_TAPS)
    res = run_bass_kernel_spmd(nc, in_maps, core_ids=list(range(8)))
    out = np.empty((B, T, D), dtype=np.float32)
    for core in range(8):
        b, half = core // 2, core % 2
        o = np.asarray(res.results[core]["out"], dtype=np.float32)
        out[b, half * TOK:(half + 1) * TOK, :] = o.T
    return out


# version 13
# speedup vs baseline: 1.0734x; 1.0502x over previous
import numpy as np
from contextlib import ExitStack

import concourse.bass as bass
import concourse.mybir as mybir
from concourse.bass_utils import run_bass_kernel_spmd

F32 = mybir.dt.float32
BF16 = mybir.dt.bfloat16
AF = mybir.ActivationFunctionType
ALU = mybir.AluOpType

B, T, D = 4, 4096, 2048
H, HD, HHD = 16, 128, 64
HL = 8
SCALE = 0.125
EBIAS = -40.0
HID = 8192
TOK = 2048
NB = T // 128
LN_EPS = 1e-5
NEG = -1.0e30
FFN_TP = [512, 512, 512, 512]
ENGINES = ("tensor", "scalar", "vector", "gpsimd", "sync")

DEBUG_TAPS = []
PHASE_LIMIT = 4
SKIP_COLLECTIVE = False
DEBUG_FIN = None
P2_TRUNC = None
P1_BLOCKS = 8


class Sem:
    def __init__(self, stack, nc, name):
        self.h = stack.enter_context(nc.semaphore(name))
        self.n = 0
        self.name = name


class Phase:

    def __init__(self, nc):
        self.nc = nc
        self.ops = []

    def op(self, eng, fn, waits=(), incs=()):
        waits = [(s, v) for (s, v) in waits]
        incs = [(s, k) for (s, k) in incs]
        for s, k in incs:
            s.n += k
        self.ops.append((eng, fn, waits, incs))

    def wait(self, eng, sem, val):
        self.ops.append((eng, None, [(sem, val)], []))

    def barrier_inc(self, sbar):
        for e in ENGINES:
            self.ops.append((e, "BARINC", [], [(sbar, 1)]))
        sbar.n += 0

    def barrier_wait(self, sbar, val):
        for e in ENGINES:
            self.ops.append((e, None, [(sbar, val)], []))

    def emit(self):
        with self.nc.Block() as block:
            def run(eng_name, eng):
                for e, fn, waits, incs in self.ops:
                    if e != eng_name:
                        continue
                    for s, v in waits:
                        eng.wait_ge(s.h, v)
                    if fn is None:
                        continue
                    if fn == "BARINC":
                        for s, k in incs:
                            eng.sem_inc(s.h, k)
                        continue
                    ins = fn(eng)
                    for s, k in incs:
                        ins.then_inc(s.h, k)

            @block.tensor
            def _(t):
                run("tensor", t)

            @block.scalar
            def _(sc):
                run("scalar", sc)

            @block.vector
            def _(v):
                run("vector", v)

            @block.gpsimd
            def _(gp):
                run("gpsimd", gp)

            @block.sync
            def _(sy):
                run("sync", sy)


def ssync(eng, sem, ins):
    sem.n += 1
    ins.then_inc(sem.h, 1)
    eng.wait_ge(sem.h, sem.n)


class Ctx:
    pass


def build(nc, debug_taps=(), phase_limit=5):
    c = Ctx()
    c.nc = nc
    dt = nc.dram_tensor
    c.xT = dt("xT", [D, T], F32, kind="ExternalInput")
    c.wq = dt("wq", [D, HL * HD], F32, kind="ExternalInput")
    c.wk = dt("wk", [D, HL * HD], F32, kind="ExternalInput")
    c.wv = dt("wv", [D, HL * HD], F32, kind="ExternalInput")
    c.wo = dt("wo", [H * HD, D], F32, kind="ExternalInput")
    c.lam = dt("lam", [1, HL], F32, kind="ExternalInput")
    c.wgT = dt("wgT", [D, HID], F32, kind="ExternalInput")
    c.wuT = dt("wuT", [D, HID], F32, kind="ExternalInput")
    c.wdT = dt("wdT", [HID, D], F32, kind="ExternalInput")
    c.maskc = dt("maskc", [128, 128], F32, kind="ExternalInput")
    c.ones_row = dt("ones_row", [1, 128], F32, kind="ExternalInput")
    c.ones_col = dt("ones_col", [128, 1], F32, kind="ExternalInput")
    c.idt = dt("idt", [128, 128], F32, kind="ExternalInput")
    c.xresT = dt("xresT", [D, TOK], F32, kind="ExternalInput")
    c.cvec = dt("cvec", [128, 2], F32, kind="ExternalInput")
    c.out = dt("out", [D, TOK], F32, kind="ExternalOutput")
    c.Kd = dt("Kd", [HL * HD, T], BF16)
    c.Qd = dt("Qd", [HL * HD, T], BF16)
    c.Vd = dt("Vd", [NB, 128, HL * 129], BF16)
    c.ACCd = [dt(f"ACCd{g}", [T, 4 * HD], BF16) for g in range(2)]
    c.ACCg = [dt(f"ACCg{g}", [2 * T, 4 * HD], BF16) for g in range(2)]
    c.ACCl = dt("ACCl", [TOK, H * HD], BF16)
    c.Y1f = dt("Y1f", [D, TOK], F32)
    c.Y1b = dt("Y1b", [D, TOK], BF16)
    c.Y2f = dt("Y2f", [D, TOK], F32)
    c.wgB = dt("wgB", [64, 128, 2048], BF16)
    c.wuB = dt("wuB", [64, 128, 2048], BF16)
    c.wdB = dt("wdB", [16, 128, 8192], BF16)

    with ExitStack() as top:
        E = top.enter_context
        c.lamneg = E(nc.sbuf_tensor("lamneg", [128, HL], F32))
        c.idt_bf = E(nc.sbuf_tensor("idt_bf", [128, 128], BF16))
        c.mask_sb = E(nc.sbuf_tensor("mask_sb", [128, 128], F32))
        c.onesr = E(nc.sbuf_tensor("onesr", [1, 128], F32))
        c.onesc = E(nc.sbuf_tensor("onesc", [128, 1], F32))
        c.cvec_sb = E(nc.sbuf_tensor("cvec_sb", [128, 2], F32))
        for nm in ("sBAR", "sW", "sX", "sPE", "sEVK", "sEVV", "sSTK", "sSTV",
                   "sMISC", "s2L", "s2PS", "s2MK", "s2EX", "s2AV", "s2FI",
                   "s2AS", "sCC", "s3L", "s3XR", "s3WO", "s3TP", "s3DV",
                   "s3EO", "s3SQ", "s3ST", "s3SM", "s3AC", "s3MM", "s3Y1",
                   "s3SO", "s3SB", "s4YL", "s4WL", "s4PE", "s4AC", "s4DV",
                   "s4WL2", "s4PE2", "s4RL", "s4DV2", "s4SO", "s5L", "sVL", "s5SQ",
                   "s5ST", "s5SM", "s5AC", "s5MM", "s5Y", "s5SO", "sVS",
                   "sSS", "sLC", "s2FB", "s4WP"):
            setattr(c, nm, Sem(top, nc, nm))
        c.sXs = [Sem(top, nc, f"sXs{i}") for i in range(2)]
        c.sSTKs = [Sem(top, nc, f"sSTKs{i}") for i in range(4)]
        c.sSTVs = [Sem(top, nc, f"sSTVs{i}") for i in range(2)]
        c.s2ASs = [Sem(top, nc, f"s2ASs{i}") for i in range(4)]
        c.s3Ls = [Sem(top, nc, f"s3Ls{i}") for i in range(2)]
        c.s3LBs = [Sem(top, nc, f"s3LBs{i}") for i in range(2)]
        c.s3XRs = [Sem(top, nc, f"s3XRs{i}") for i in range(2)]
        c.s4WLs = [Sem(top, nc, f"s4WLs{i}") for i in range(2)]
        c.s4WL2s = [Sem(top, nc, f"s4WL2s{i}") for i in range(2)]
        c.s4RLs = [Sem(top, nc, f"s4RLs{i}") for i in range(2)]
        c.s4SOs = [Sem(top, nc, f"s4SOs{i}") for i in range(2)]
        c.s4NLs = [Sem(top, nc, f"s4NLs{i}") for i in range(2)]

        phases = [_p1, _p2, _p3, _p4]
        for i in range(min(phase_limit, 4)):
            phases[i](c, top)
        _taps(c, phase_limit, debug_taps)
    return nc


def _taps(c, nphases, debug_taps):
    if not debug_taps:
        return
    nc = c.nc
    P = Phase(nc)
    P.barrier_wait(c.sBAR, 5 * nphases)
    sT = c.s5SO
    for nm in debug_taps:
        src = getattr(c, nm)

        def ftap(gp, src=src, nm=nm):
            tout = nc.dram_tensor(f"tap_{nm}", list(src.ap().shape),
                                  F32, kind="ExternalOutput")
            return gp.dma_start(out=tout.ap(), in_=src.ap())
        P.op("gpsimd", ftap, incs=[(sT, 16)])
    P.wait("gpsimd", sT, sT.n)
    P.emit()


def _p1(c, top):
    nc = c.nc
    with ExitStack() as st:
        E = st.enter_context
        wq_sb = E(nc.sbuf_tensor("wq_sb", [128, 16 * 1024], BF16))
        wk_sb = E(nc.sbuf_tensor("wk_sb", [128, 16 * 1024], BF16))
        wv_sb = E(nc.sbuf_tensor("wv_sb", [128, 16 * 1024], BF16))
        xb = [E(nc.sbuf_tensor(f"xb{i}", [128, 16 * 512], BF16)) for i in range(2)]
        kq_st = [E(nc.sbuf_tensor(f"kqst{i}", [128, 512], BF16)) for i in range(4)]
        v_st = [E(nc.sbuf_tensor(f"vst{i}", [128, 4 * 1032], BF16)) for i in range(2)]
        lam_sb = E(nc.sbuf_tensor("lam_sb", [1, HL], F32))
        lam_t1 = E(nc.sbuf_tensor("lam_t1", [1, HL], F32))
        lam_t2 = E(nc.sbuf_tensor("lam_t2", [1, HL], F32))
        psp = [E(nc.psum_tensor(f"psp{i}", [128, 512], F32)) for i in range(4)]
        ps_lam = E(nc.psum_tensor("ps_lam", [128, HL], F32))

        P = Phase(nc)
        W_SB = {"q": wq_sb, "k": wk_sb, "v": wv_sb}
        KD = {"q": c.Qd, "k": c.Kd}

        def wload(dst, src):
            return lambda g: g.dma_start(
                out=dst.ap().rearrange("p (c h) -> p c h", c=16),
                in_=src.ap().rearrange("(c p) h -> p c h", p=128))
        P.op("gpsimd", wload(wq_sb, c.wq), incs=[(c.sW, 16)])
        P.op("gpsimd", wload(wk_sb, c.wk), incs=[(c.sW, 16)])
        P.op("gpsimd", wload(wv_sb, c.wv), incs=[(c.sW, 16)])
        P.op("gpsimd", lambda g: g.dma_start(out=c.idt_bf[:, :], in_=c.idt[:, :]),
             incs=[(c.sW, 16)])
        P.op("gpsimd", lambda g: g.dma_start(out=c.mask_sb[:, :], in_=c.maskc[:, :]),
             incs=[(c.sW, 16)])
        P.op("gpsimd", lambda g: g.dma_start(out=c.onesr[:, :], in_=c.ones_row[:, :]),
             incs=[(c.sW, 16)])
        P.op("gpsimd", lambda g: g.dma_start(out=c.onesc[:, :], in_=c.ones_col[:, :]),
             incs=[(c.sW, 16)])
        P.op("gpsimd", lambda g: g.dma_start(out=lam_sb[:, :], in_=c.lam[:, :]),
             incs=[(c.sW, 16)])
        P.op("gpsimd", lambda g: g.dma_start(out=c.cvec_sb[:, :], in_=c.cvec[:, :]),
             incs=[(c.sW, 16)])
        WTOT = c.sW.n

        P.op("vector", lambda v, s=0: v.memset(v_st[s].ap(), 1.0), incs=[(c.sMISC, 1)])
        P.op("vector", lambda v, s=1: v.memset(v_st[s].ap(), 1.0), incs=[(c.sMISC, 1)])
        VONES = c.sMISC.n

        P.op("scalar", lambda s: s.activation(lam_t1[:, :], lam_sb[:, :], AF.Exp,
                                              scale=-1.0),
             waits=[(c.sW, WTOT)], incs=[(c.sMISC, 1)])
        P.op("vector", lambda v: v.tensor_scalar_add(lam_t2[:, :], lam_t1[:, :], 1.0),
             waits=[(c.sMISC, VONES + 1)], incs=[(c.sMISC, 1)])
        P.op("vector", lambda v: v.reciprocal(lam_t1[:, :], lam_t2[:, :]),
             waits=[(c.sMISC, c.sMISC.n)], incs=[(c.sMISC, 1)])
        P.op("vector", lambda v: v.tensor_scalar_mul(lam_t2[:, :], lam_t1[:, :], -1.0),
             waits=[(c.sMISC, c.sMISC.n)], incs=[(c.sMISC, 1)])
        LAMR = c.sMISC.n
        P.op("tensor", lambda t: t.matmul(ps_lam[:, :], c.onesr[:, :], lam_t2[:, :],
                                          start=True, stop=True),
             waits=[(c.sMISC, LAMR)], incs=[(c.sMISC, 1)])
        P.op("vector", lambda v: v.tensor_copy(c.lamneg[:, :], ps_lam[:, :]),
             waits=[(c.sMISC, c.sMISC.n)], incs=[(c.sMISC, 1)])

        uses = ([("k", hl) for hl in range(HL)] + [("q", hl) for hl in range(HL)]
                + [("v", j2, hc) for j2 in range(4) for hc in range(2)])
        NU = len(uses)
        evac_of = {}
        kq_i = 0
        for bi in range(P1_BLOCKS):
            def xload(g, bi=bi):
                return g.dma_start(
                    out=xb[bi % 2].ap().rearrange("p (cc t) -> p cc t", cc=16),
                    in_=c.xT.ap()[:, bi * 512:(bi + 1) * 512]
                        .rearrange("(cc p) t -> p cc t", p=128))
            xw = []
            if bi >= 2:
                xw.append((c.sPE, NU * (bi - 1)))
            P.op("gpsimd", xload, waits=xw, incs=[(c.sXs[bi % 2], 16)])

            for ui, use in enumerate(uses):
                idx = bi * NU + ui
                ps = psp[idx % 4]
                mmw = [(c.sXs[bi % 2], 16 * (bi // 2 + 1))]
                if bi == 0 and ui == 0:
                    mmw.append((c.sW, WTOT))
                if idx >= 4:
                    mmw.append(evac_of[idx - 4])

                if use[0] in ("k", "q"):
                    kind, hl = use

                    def mm(t, ps=ps, kind=kind, hl=hl, x=xb[bi % 2]):
                        w = W_SB[kind]
                        for cc in range(16):
                            ins = t.matmul(
                                ps[:, :],
                                w[:, cc * 1024 + hl * 128:cc * 1024 + hl * 128 + 128],
                                x[:, cc * 512:(cc + 1) * 512],
                                start=(cc == 0), stop=(cc == 15))
                        return ins
                    P.op("tensor", mm, waits=mmw, incs=[(c.sPE, 1)])
                    stg = kq_st[kq_i % 4]
                    evw = [(c.sPE, idx + 1)]
                    if kq_i >= 4:
                        evw.append((c.sSTKs[kq_i % 4], 16 * (kq_i // 4)))

                    def ev(s, stg=stg, ps=ps):
                        return s.copy(stg[:, :], ps[:, :])
                    P.op("scalar", ev, waits=evw, incs=[(c.sEVK, 1)])
                    evac_of[idx] = (c.sEVK, c.sEVK.n)
                    dram = KD[kind]

                    def stf(sy, dram=dram, hl=hl, bi=bi, stg=stg):
                        return sy.dma_start(
                            out=dram.ap()[hl * 128:(hl + 1) * 128,
                                          bi * 512:(bi + 1) * 512],
                            in_=stg[:, :])
                    P.op("sync", stf, waits=[(c.sEVK, c.sEVK.n)],
                         incs=[(c.sSTKs[kq_i % 4], 16)])
                    kq_i += 1
                else:
                    _, j2, hc = use

                    def mm(t, ps=ps, j2=j2, hc=hc, x=xb[bi % 2]):
                        for cc in range(16):
                            ins = t.matmul(
                                ps[:, :],
                                x[:, cc * 512 + j2 * 128:cc * 512 + j2 * 128 + 128],
                                wv_sb[:, cc * 1024 + hc * 512:cc * 1024 + (hc + 1) * 512],
                                start=(cc == 0), stop=(cc == 15))
                        return ins
                    P.op("tensor", mm, waits=mmw, incs=[(c.sPE, 1)])
                    stg = v_st[bi % 2]
                    evw = [(c.sPE, idx + 1)]
                    if bi >= 2 and ui == 16:
                        evw.append((c.sSTVs[bi % 2], 16 * (bi // 2)))

                    def ev(v, stg=stg, ps=ps, j2=j2, hc=hc):
                        dst = stg.ap().rearrange("p (j h d) -> p j h d", j=4, h=HL)
                        dst = dst[:, j2, hc * 4:(hc + 1) * 4, 0:128]
                        return v.tensor_copy(
                            dst, ps.ap().rearrange("p (h d) -> p h d", h=4))
                    P.op("vector", ev, waits=evw, incs=[(c.sEVV, 1)])
                    evac_of[idx] = (c.sEVV, c.sEVV.n)

            def vstf(sy, bi=bi, stg=v_st[bi % 2]):
                return sy.dma_start(
                    out=c.Vd.ap()[4 * bi:4 * bi + 4, :, :]
                        .rearrange("j p h -> p j h"),
                    in_=stg.ap().rearrange("p (j h) -> p j h", j=4))
            P.op("sync", vstf, waits=[(c.sEVV, c.sEVV.n)],
                 incs=[(c.sSTVs[bi % 2], 16)])

        for sm in c.sSTKs + c.sSTVs:
            P.wait("sync", sm, sm.n)
        P.barrier_inc(c.sBAR)
        P.emit()


def _p2(c, top):
    nc = c.nc
    with ExitStack() as st:
        E = st.enter_context
        k_sb = E(nc.sbuf_tensor("k_sb", [128, 4 * T], BF16))
        q_sb = E(nc.sbuf_tensor("q_sb", [128, 4 * T], BF16))
        v_sb = E(nc.sbuf_tensor("v_sb", [128, NB * 4 * 129], BF16))
        p_sb = [E(nc.sbuf_tensor(f"p_sb{i}", [128, 1024], BF16)) for i in range(4)]
        acc_st = [E(nc.sbuf_tensor(f"accst{i}", [128, 512], BF16)) for i in range(4)]
        fin_a1 = E(nc.sbuf_tensor("fin_a1", [128, 4 * 128], F32))
        fin_a2 = E(nc.sbuf_tensor("fin_a2", [128, 4 * 128], F32))
        fin_u = E(nc.sbuf_tensor("fin_u", [128, 4 * 128], F32))
        fr1 = E(nc.sbuf_tensor("fr1", [128, 4], F32))
        fr2 = E(nc.sbuf_tensor("fr2", [128, 4], F32))
        ft1 = E(nc.sbuf_tensor("ft1", [128, 4], F32))
        fkn = E(nc.sbuf_tensor("fkn", [128, 4], F32))
        psS = [E(nc.psum_tensor(f"psS{i}", [128, 1024], F32)) for i in range(2)]
        accP = E(nc.psum_tensor("accP", [128, 2048], F32))

        def a1col(stt):
            return (stt // 2) * 512 + (stt % 2) * 130

        def a2col(stt):
            return 1024 + a1col(stt)

        P = Phase(nc)
        P.barrier_wait(c.sBAR, 5)

        events = []
        runs = []
        for g in range(2):
            for ig in range(8):
                for hl in range(4):
                    if P2_TRUNC is not None and len(runs) >= P2_TRUNC:
                        continue
                    r = len(runs)
                    t0 = ig * 4
                    ev0 = len(events)
                    for j in range(t0 + 4):
                        events.append((g, ig, hl, j, r))
                    runs.append((g, ig, hl, ev0, len(events)))
        NEV = len(events)
        NRUN = len(runs)

        av_after_group = {}
        n = 0
        for g in range(2):
            for _, ig, hl, j, r in events:
                pass
        av_cnt = []
        tot = 0
        for (g, ig, hl, j, r) in events:
            tot += 1
            av_cnt.append(tot)
        g_end_ev = {0: 0, 1: 0}
        first_ev_of_g = {}
        for i, (g, ig, hl, j, r) in enumerate(events):
            g_end_ev[g] = i + 1
            if g not in first_ev_of_g:
                first_ev_of_g[g] = i
        ps_cnt = list(range(1, NEV + 1))
        ex_cnt = list(range(1, NEV + 1))
        mk_cnt = []
        mtot = 0
        for (g, ig, hl, j, r) in events:
            if j >= ig * 4:
                mtot += 1
            mk_cnt.append(mtot)

        def gloadKQ(g):
            def f(sy, g=g):
                sy.dma_start(
                    out=k_sb.ap().rearrange("p (h t) -> p h t", h=4),
                    in_=c.Kd.ap()[g * 512:(g + 1) * 512, :]
                        .rearrange("(h p) t -> p h t", p=128)
                ).then_inc(c.s2L.h, 16)
                return sy.dma_start(
                    out=q_sb.ap().rearrange("p (h t) -> p h t", h=4),
                    in_=c.Qd.ap()[g * 512:(g + 1) * 512, :]
                        .rearrange("(h p) t -> p h t", p=128))
            return f

        def gloadV(g):
            def f(sy, g=g):
                return sy.dma_start(
                    out=v_sb.ap().rearrange("p (b h) -> p b h", b=NB),
                    in_=c.Vd.ap()[:, :, g * 516:(g + 1) * 516]
                        .rearrange("b p h -> p b h"))
            return f

        def mk_scores(i):
            g, ig, hl, j, r = events[i]
            t0 = ig * 4
            diag = j >= t0
            co = 128 * max(0, j - t0)
            b = i % 2

            def f(t):
                t.matmul(psS[b][:, co:512],
                         k_sb[0:64, hl * T + j * 128:hl * T + j * 128 + 128],
                         q_sb[0:64, hl * T + ig * 512 + co:hl * T + (ig + 1) * 512],
                         start=not diag, stop=True, tile_position=(0, 0),
                         skip_group_check=diag)
                return t.matmul(
                    psS[b][:, 512 + co:1024],
                    k_sb[64:128, hl * T + j * 128:hl * T + j * 128 + 128],
                    q_sb[64:128, hl * T + ig * 512 + co:hl * T + (ig + 1) * 512],
                    start=not diag, stop=True, tile_position=(64, 0),
                    skip_group_check=diag)
            w = []
            if i >= 2:
                w.append((c.s2EX, i - 1))
            w.append((c.s2L, 32 * (g + 1)))
            if j >= t0:
                w.append((c.s2MK, mk_cnt[i]))
            return f, w

        def mk_mask(i):
            g, ig, hl, j, r = events[i]
            t0 = ig * 4
            sd = j - t0
            co = 128 * sd
            b = i % 2

            def f(v):
                ins = v.tensor_copy(psS[b][:, co:co + 128], c.mask_sb[:, :])
                ins = v.tensor_copy(psS[b][:, 512 + co:512 + co + 128],
                                    c.mask_sb[:, :])
                if co + 128 < 512:
                    ins = v.memset(psS[b][:, co + 128:512], 0.0)
                    ins = v.memset(psS[b][:, 512 + co + 128:1024], 0.0)
                return ins
            w = []
            if i >= 2:
                w.append((c.s2EX, i - 1))
            return f, w

        def mk_exp(i):
            b = i % 2
            g, ig, hl, j, r = events[i]
            t0 = ig * 4
            co = 128 * max(0, j - t0)

            def f(s):
                pv = psS[b].ap().rearrange("p (two n) -> p two n", two=2)[:, :, co:512]
                ov = p_sb[i % 4].ap().rearrange("p (two n) -> p two n",
                                                two=2)[:, :, co:512]
                return s.activation(ov, pv, AF.Exp,
                                    bias=c.cvec_sb[:, 0:1], scale=SCALE)
            w = [(c.s2PS, ps_cnt[i])]
            if i >= 4:
                w.append((c.s2AV, av_cnt[i - 4]))
            return f, w

        def mk_attnv(i):
            g, ig, hl, j, r = events[i]
            t0 = ig * 4
            st0 = max(0, j - t0)
            b = i % 4
            lhl = hl

            def f(t):
                for stt in range(st0, 4):
                    c1, c2 = a1col(stt), a2col(stt)
                    first = (j == 0) and stt in (0, 2)
                    vrhs = v_sb[:, j * 516 + lhl * 129:j * 516 + (lhl + 1) * 129]
                    t.matmul(accP[:, c1:c1 + 129],
                             p_sb[b][:, stt * 128:(stt + 1) * 128], vrhs,
                             start=first, stop=False, skip_group_check=True)
                    ins = t.matmul(accP[:, c2:c2 + 129],
                                   p_sb[b][:, 512 + stt * 128:512 + (stt + 1) * 128],
                                   vrhs, start=first, stop=False,
                                   skip_group_check=True)
                return ins
            w = [(c.s2EX, ex_cnt[i])]
            if j == 0 and r >= 1:
                w.append((c.s2FI, 2 * r))
            if i == first_ev_of_g[g]:
                w.append((c.sVL, 16 * (g + 1)))
            return f, w

        ACC_FREE = 2048

        def av(handle, offset, dims, free):
            return bass.AP(handle, offset, [[free, 128]] + [list(d) for d in dims])

        def mk_finA(r):
            g, ig, hl, ev0, ev1 = runs[r]

            def f1(v):
                if DEBUG_FIN:
                    stg = acc_st[r % 4]
                    for stt in range(4):
                        c1, c2 = a1col(stt), a2col(stt)
                        if DEBUG_FIN == "acc1":
                            ins = v.tensor_copy(stg[:, stt * 128:(stt + 1) * 128],
                                                accP[:, c1:c1 + 128])
                        elif DEBUG_FIN == "s1":
                            v.memset(stg[:, stt * 128:(stt + 1) * 128], 0.0)
                            v.tensor_copy(stg[:, stt * 128:stt * 128 + 1],
                                          accP[:, c1 + 128:c1 + 129])
                            ins = v.tensor_copy(stg[:, stt * 128 + 1:stt * 128 + 2],
                                                accP[:, c2 + 128:c2 + 129])
                    return ins
                sdim = [(512, 2), (130, 2), (1, 1)]
                fdim = [(2, 2), (1, 2), (1, 1)]
                ins = v.reciprocal(av(fr1, 0, fdim, 4), av(accP, 128, sdim, ACC_FREE))
                ins = v.reciprocal(av(fr2, 0, fdim, 4), av(accP, 1024 + 128, sdim, ACC_FREE))
                ssync(v, c.sVS, ins)
                ins = v.tensor_tensor(av(ft1, 0, fdim, 4),
                                      av(accP, 128, sdim, ACC_FREE),
                                      av(fr2, 0, fdim, 4), ALU.mult)
                cdim = [(512, 2), (130, 2), (1, 128)]
                odim = [(256, 2), (128, 2), (1, 128)]
                v.tensor_copy(av(fin_a1, 0, odim, 512), av(accP, 0, cdim, ACC_FREE))
                return v.tensor_copy(av(fin_a2, 0, odim, 512),
                                     av(accP, 1024, cdim, ACC_FREE))
            return f1, [(c.s2AV, av_cnt[ev1 - 1])]

        def mk_finB(r):
            g, ig, hl, ev0, ev1 = runs[r]
            stg = acc_st[r % 4]
            lamcol = g * 4 + hl

            def f1(v):
                if DEBUG_FIN:
                    return v.engine_nop()
                ins = v.tensor_scalar_mul(fkn[:, 0:4], ft1[:, 0:4],
                                          c.lamneg[:, lamcol:lamcol + 1])
                ssync(v, c.sVS, ins)
                for stt in range(4):
                    ins = v.scalar_tensor_tensor(fin_u[:, stt * 128:(stt + 1) * 128],
                                                 fin_a2[:, stt * 128:(stt + 1) * 128],
                                                 fkn[:, stt:stt + 1],
                                                 fin_a1[:, stt * 128:(stt + 1) * 128],
                                                 ALU.mult, ALU.add)
                ssync(v, c.sVS, ins)
                for stt in range(4):
                    ins = v.tensor_scalar_mul(stg[:, stt * 128:(stt + 1) * 128],
                                              fin_u[:, stt * 128:(stt + 1) * 128],
                                              fr1[:, stt:stt + 1])
                return ins
            w = [(c.s2FI, 2 + 2 * r)]
            if r >= 4:
                w.append((c.s2ASs[r % 4], 16 * (r // 4)))
            return f1, w

        def mk_accstore(r):
            g, ig, hl, ev0, ev1 = runs[r]
            stg = acc_st[r % 4]

            def f(gp, g=g):
                return gp.dma_start(
                    out=c.ACCd[g].ap()[ig * 512:(ig + 1) * 512,
                                       hl * 128:(hl + 1) * 128]
                        .rearrange("(s p) d -> p s d", p=128),
                    in_=stg.ap().rearrange("p (s d) -> p s d", s=4))

            return f, [(c.s2FB, r + 1)], r

        preps = []
        for sgrp in range(64):
            def fprep(g, sgrp=sgrp):
                g.dma_start(
                    out=c.wgB.ap()[sgrp],
                    in_=c.wgT.ap()[:, sgrp * 128:(sgrp + 1) * 128]
                        .rearrange("(cc p) h -> p cc h", p=128)
                ).then_inc(c.s4WP.h, 16)
                return g.dma_start(
                    out=c.wuB.ap()[sgrp],
                    in_=c.wuT.ap()[:, sgrp * 128:(sgrp + 1) * 128]
                        .rearrange("(cc p) h -> p cc h", p=128))
            preps.append((fprep, 32))
        for dcg in range(16):
            for hh in range(2):
                def fprepd(g, dcg=dcg, hh=hh):
                    return g.dma_start(
                        out=c.wdB.ap()[dcg][:, hh * 4096:(hh + 1) * 4096],
                        in_=c.wdT.ap()[hh * 4096:(hh + 1) * 4096,
                                       dcg * 128:(dcg + 1) * 128]
                            .rearrange("(sp p) d -> p sp d", p=128))
                preps.append((fprepd, 16))

        def emit_preps(k):
            while preps and k > 0:
                fn, inc = preps.pop(0)
                c.s4WP.n += inc - 16
                P.op("gpsimd", fn, incs=[(c.s4WP, 16)])
                k -= 1

        def emit_gather(g):
            need = 16 * (8 if g == 0 else 16)
            if SKIP_COLLECTIVE:
                def coll(gp, g=g):
                    gp.dma_start(out=c.ACCg[g].ap()[0:T, :],
                                 in_=c.ACCd[g].ap()).then_inc(c.sCC.h, 16)
                    return gp.dma_start(out=c.ACCg[g].ap()[T:2 * T, :],
                                        in_=c.ACCd[g].ap())
                c.sCC.n += 16
                P.op("gpsimd", coll, waits=[(sm, need) for sm in c.s2ASs],
                     incs=[(c.sCC, 16)])
            else:
                def coll(gp, g=g):
                    gp.collective_compute(
                        "AllGather", ALU.bypass,
                        replica_groups=[[0, 1], [2, 3], [4, 5], [6, 7]],
                        ins=[c.ACCd[g].ap().opt()], outs=[c.ACCg[g].ap().opt()]
                    ).then_inc(c.sCC.h)
                    return gp.nop()
                c.sCC.n += 1
                P.op("gpsimd", coll, waits=[(sm, need) for sm in c.s2ASs],
                     incs=[])
            cc_val = c.sCC.n

            def localize(sy, g=g):
                pid = sy.partition_id()
                with sy.register(f"rh{g}") as rh:
                    sy.reg_mod(rh, pid, 2)
                    for half in (0, 1):
                        with sy.If_eq(rh, half):
                            r0 = half * TOK
                            sy.dma_start(
                                out=c.ACCl.ap()[:, g * 512:(g + 1) * 512],
                                in_=c.ACCg[g].ap()[r0:r0 + TOK, :]
                            ).then_inc(c.sLC.h, 16)
                            sy.dma_start(
                                out=c.ACCl.ap()[:, 1024 + g * 512:
                                                1024 + (g + 1) * 512],
                                in_=c.ACCg[g].ap()[T + r0:T + r0 + TOK, :]
                            ).then_inc(c.sLC.h, 16)
                return sy.nop()
            if g == 0:
                c.sLC.n += 32
                P.op("sync", localize, waits=[(c.sCC, cc_val)], incs=[])
            else:
                c.loc1 = (localize, cc_val)

        c.s2L.n += 16
        P.op("sync", gloadKQ(0), incs=[(c.s2L, 16)])
        P.op("sync", gloadV(0), incs=[(c.sVL, 16)])
        if g_end_ev[1] > g_end_ev[0]:
            c.s2L.n += 16
            P.op("sync", gloadKQ(1), waits=[(c.s2PS, ps_cnt[g_end_ev[0] - 1])],
                 incs=[(c.s2L, 16)])
            P.op("sync", gloadV(1), waits=[(c.s2AV, av_cnt[g_end_ev[0] - 1])],
                 incs=[(c.sVL, 16)])

        def emit_attnv(i):
            g, ig, hl, j, r = events[i]
            fn, w = mk_attnv(i)
            P.op("tensor", fn, waits=w, incs=[(c.s2AV, 1)])
            if j == ig * 4 + 3:
                fn, w = mk_finA(r)
                P.op("vector", fn, waits=w, incs=[(c.s2FI, 2)])
                fn, w = mk_finB(r)
                P.op("vector", fn, waits=w, incs=[(c.s2FB, 1)])
                fn, w, rr = mk_accstore(r)
                P.op("gpsimd", fn, waits=w, incs=[(c.s2ASs[rr % 4], 16)])
                emit_preps(2)
                if r == NRUN // 2 - 1:
                    emit_gather(0)

        delayed = None
        for i in range(NEV):
            g, ig, hl, j, r = events[i]
            t0 = ig * 4
            if delayed is not None and events[delayed][0] != g:
                emit_attnv(delayed)
                delayed = None
            if j >= t0:
                fn, w = mk_mask(i)
                P.op("vector", fn, waits=w, incs=[(c.s2MK, 1)])
            fn, w = mk_scores(i)
            P.op("tensor", fn, waits=w, incs=[(c.s2PS, 1)])
            fn, w = mk_exp(i)
            P.op("scalar", fn, waits=w, incs=[(c.s2EX, 1)])
            if delayed is not None:
                emit_attnv(delayed)
            delayed = i
        if delayed is not None:
            emit_attnv(delayed)

        emit_preps(len(preps))
        P.barrier_inc(c.sBAR)
        emit_gather(1)
        P.emit()


def _p3(c, top):
    nc = c.nc
    with ExitStack() as st:
        E = st.enter_context
        wo_sb = E(nc.sbuf_tensor("wo_sb", [128, 16 * 2048], BF16))
        atm = [E(nc.sbuf_tensor(f"atm{i}", [128, 4 * 2048], BF16)) for i in range(2)]
        accT = E(nc.sbuf_tensor("accT", [128, 16 * 512], BF16))
        xres = [E(nc.sbuf_tensor(f"xres{i}", [128, 512], F32)) for i in range(2)]
        y1 = E(nc.sbuf_tensor("y1", [128, 16 * 512], F32))
        sq = [E(nc.sbuf_tensor(f"sq3{i}", [128, 512], F32)) for i in range(2)]
        rows = E(nc.sbuf_tensor("rows3", [1, 4 * 512], F32))
        tmp = [E(nc.sbuf_tensor(f"tmp3{i}", [128, 512], F32)) for i in range(2)]
        ps_t = [E(nc.psum_tensor(f"ps_t{i}", [128, 1024], BF16)) for i in range(2)]
        ps_o = [E(nc.psum_tensor(f"ps_o{i}", [128, 512], F32)) for i in range(2)]
        ps_sum = E(nc.psum_tensor("ps_sum", [128, 512], F32))
        ps_ssq = E(nc.psum_tensor("ps_ssq", [128, 512], F32))
        ps_mu = E(nc.psum_tensor("ps_mu", [128, 512], F32))
        ps_rs = E(nc.psum_tensor("ps_rs", [128, 512], F32))

        P = Phase(nc)
        P.barrier_wait(c.sBAR, 10)

        def woload(gp):
            return gp.dma_start(
                out=wo_sb.ap().rearrange("p (hc d) -> p hc d", hc=16),
                in_=c.wo.ap().rearrange("(hc p) d -> p hc d", p=128))
        P.op("gpsimd", woload, incs=[(c.s3WO, 16)])

        A_H = (0, 1, 2, 3, 8, 9, 10, 11)
        B_H = (4, 5, 6, 7, 12, 13, 14, 15)

        def emit_aloads(tb):
            def aloadA(sy, tb=tb):
                av = atm[tb % 2].ap().rearrange("p (ts h) -> p ts h", ts=4)
                sy.dma_start(
                    out=av[:, :, 0:512],
                    in_=c.ACCl.ap()[tb * 512:(tb + 1) * 512, 0:512]
                        .rearrange("(ts p) h -> p ts h", p=128)
                ).then_inc(c.s3Ls[tb % 2].h, 16)
                return sy.dma_start(
                    out=av[:, :, 1024:1536],
                    in_=c.ACCl.ap()[tb * 512:(tb + 1) * 512, 1024:1536]
                        .rearrange("(ts p) h -> p ts h", p=128))
            lw = [(c.sLC, 32)]
            if tb >= 2:
                lw.append((c.s3TP, 64 * (tb - 1)))
            c.s3Ls[tb % 2].n += 16
            P.op("sync", aloadA, waits=lw, incs=[(c.s3Ls[tb % 2], 16)])
            if tb == 0:
                fnloc, ccv = c.loc1
                c.sLC.n += 32
                P.op("sync", fnloc, waits=[(c.sCC, ccv)], incs=[])

            def aloadB(sy, tb=tb):
                av = atm[tb % 2].ap().rearrange("p (ts h) -> p ts h", ts=4)
                sy.dma_start(
                    out=av[:, :, 512:1024],
                    in_=c.ACCl.ap()[tb * 512:(tb + 1) * 512, 512:1024]
                        .rearrange("(ts p) h -> p ts h", p=128)
                ).then_inc(c.s3LBs[tb % 2].h, 16)
                return sy.dma_start(
                    out=av[:, :, 1536:2048],
                    in_=c.ACCl.ap()[tb * 512:(tb + 1) * 512, 1536:2048]
                        .rearrange("(ts p) h -> p ts h", p=128))
            lw = [(c.sLC, 64)]
            if tb >= 2:
                lw.append((c.s3TP, 64 * (tb - 1)))
            c.s3LBs[tb % 2].n += 16
            P.op("sync", aloadB, waits=lw, incs=[(c.s3LBs[tb % 2], 16)])

        tp_i = 0
        for tb in range(4):
            if tb == 0:
                emit_aloads(0)

            for hcs, lsem in ((A_H, c.s3Ls[tb % 2]), (B_H, c.s3LBs[tb % 2])):
                for ts in range(4):
                    for hc in hcs:
                        def ftp(t, tb=tb, ts=ts, hc=hc, i=tp_i):
                            return t.transpose(
                                ps_t[i % 2][:, 0:128],
                                atm[tb % 2].ap().rearrange(
                                    "p (ts h) -> p ts h",
                                    ts=4)[:, ts, hc * 128:(hc + 1) * 128],
                                c.idt_bf[:, :])
                        w = [(lsem, 32 * (tb // 2 + 1)), (c.s3WO, 16)]
                        if tp_i >= 2:
                            w.append((c.s3DV, tp_i - 1))
                        P.op("tensor", ftp, waits=w, incs=[(c.s3TP, 1)])

                        def fev(s_, ts=ts, hc=hc, i=tp_i):
                            return s_.copy(
                                accT[:, hc * 512 + ts * 128:
                                     hc * 512 + ts * 128 + 128],
                                ps_t[i % 2][:, 0:128])
                        P.op("scalar", fev, waits=[(c.s3TP, tp_i + 1)],
                             incs=[(c.s3DV, 1)])
                        tp_i += 1

            for dc in range(16):
                def fx(gp, tb=tb, dc=dc):
                    return gp.dma_start(
                        out=xres[dc % 2][:, :],
                        in_=c.xresT.ap()[dc * 128:(dc + 1) * 128,
                                         tb * 512:(tb + 1) * 512])
                w = []
                if tb * 16 + dc >= 2:
                    w.append((c.s3EO, tb * 16 + dc - 1))
                P.op("gpsimd", fx, waits=w, incs=[(c.s3XRs[dc % 2], 16)])

                def fwo(t, dc=dc, tb=tb):
                    for hc in range(16):
                        ins = t.matmul(
                            ps_o[dc % 2][:, :],
                            wo_sb[:, hc * 2048 + dc * 128:hc * 2048 + dc * 128 + 128],
                            accT[:, hc * 512:(hc + 1) * 512],
                            start=(hc == 0), stop=(hc == 15))
                    return ins
                w = [(c.s3DV, 64 * (tb + 1))]
                if tb * 16 + dc >= 2:
                    w.append((c.s3EO, tb * 16 + dc - 1))
                if tb >= 1:
                    pass
                P.op("tensor", fwo, waits=w, incs=[(c.s3MM, 1)])

                def fres(v, dc=dc):
                    return v.tensor_tensor(y1[:, dc * 512:(dc + 1) * 512],
                                           ps_o[dc % 2][:, :], xres[dc % 2][:, :],
                                           ALU.add)
                w = [(c.s3MM, c.s3MM.n),
                     (c.s3XRs[dc % 2], 16 * ((tb * 16 + dc) // 2 + 1))]
                if tb >= 1 and dc == 0:
                    w += [(c.s3SO, 128 * tb), (c.s3SB, 128 * tb)]
                P.op("vector", fres, waits=w, incs=[(c.s3EO, 1)])

                def fsq(v, dc=dc):
                    return v.tensor_tensor(sq[dc % 2][:, :],
                                           y1[:, dc * 512:(dc + 1) * 512],
                                           y1[:, dc * 512:(dc + 1) * 512], ALU.mult)
                w = [(c.s3EO, c.s3EO.n)]
                if tb * 16 + dc >= 2:
                    w.append((c.s3ST, 2 * (tb * 16 + dc - 1)))
                P.op("vector", fsq, waits=w, incs=[(c.s3SQ, 1)])

                def fst(t, dc=dc):
                    t.matmul(ps_sum[0:1, :], c.onesc[:, :],
                             y1[:, dc * 512:(dc + 1) * 512],
                             start=(dc == 0), stop=(dc == 15))
                    return t.matmul(ps_ssq[0:1, :], c.onesc[:, :], sq[dc % 2][:, :],
                                    start=(dc == 0), stop=(dc == 15))
                w = [(c.s3SQ, c.s3SQ.n)]
                if dc == 0 and tb >= 1:
                    w.append((c.s3SM, tb))
                P.op("tensor", fst, waits=w, incs=[(c.s3ST, 2)])

            if tb < 3:
                emit_aloads(tb + 1)

            def fsm(v, tb=tb):
                v.tensor_scalar_mul(rows[:, 0:512], ps_sum[0:1, :], 1.0 / D)
                ins = v.tensor_scalar_mul(rows[:, 512:1024], ps_ssq[0:1, :], 1.0 / D)
                ssync(v, c.sVS, ins)
                ins = v.tensor_tensor(rows[:, 1024:1536], rows[:, 0:512],
                                      rows[:, 0:512], ALU.mult)
                ssync(v, c.sVS, ins)
                return v.tensor_tensor(rows[:, 1024:1536], rows[:, 512:1024],
                                       rows[:, 1024:1536], ALU.subtract)
            P.op("vector", fsm, waits=[(c.s3ST, c.s3ST.n)], incs=[(c.s3SM, 1)])

            def fac(s, tb=tb):
                ins = s.activation(rows[:, 1536:2048], rows[:, 1024:1536], AF.Ln,
                                   bias=c.cvec_sb[0:1, 1:2])
                ssync(s, c.sSS, ins)
                return s.activation(rows[:, 1536:2048], rows[:, 1536:2048], AF.Exp,
                                    scale=-0.5)
            P.op("scalar", fac, waits=[(c.s3SM, c.s3SM.n)], incs=[(c.s3AC, 1)])

            def fbc(t, tb=tb):
                t.matmul(ps_mu[:, :], c.onesr[:, :], rows[:, 0:512],
                         start=True, stop=True)
                return t.matmul(ps_rs[:, :], c.onesr[:, :], rows[:, 1536:2048],
                                start=True, stop=True)
            w = [(c.s3AC, c.s3AC.n)]
            if tb >= 1:
                w.append((c.s3Y1, 32 * tb))
            P.op("tensor", fbc, waits=w, incs=[(c.s3MM, 1)])
            bc_done = c.s3MM.n

            for dcp in range(8):
                def fnm(v, dcp=dcp):
                    for k in (0, 1):
                        dc = 2 * dcp + k
                        ins = v.tensor_tensor(tmp[k][:, :],
                                              y1[:, dc * 512:(dc + 1) * 512],
                                              ps_mu[:, :], ALU.subtract)
                    ssync(v, c.sVS, ins)
                    for k in (0, 1):
                        dc = 2 * dcp + k
                        ins = v.tensor_tensor(y1[:, dc * 512:(dc + 1) * 512],
                                              tmp[k][:, :], ps_rs[:, :],
                                              ALU.mult)
                    return ins
                w = [(c.s3MM, bc_done)]
                if dcp >= 1:
                    w.append((c.s3Y1, c.s3Y1.n))
                P.op("vector", fnm, waits=w, incs=[(c.s3Y1, 4)])

            for dcp in range(8):
                yw = 32 * tb + 4 * (dcp + 1)

                def fst1(sy, tb=tb, dcp=dcp):
                    return sy.dma_start(
                        out=c.Y1f.ap()[dcp * 256:(dcp + 1) * 256,
                                       tb * 512:(tb + 1) * 512]
                            .rearrange("(cc p) t -> p cc t", p=128),
                        in_=y1.ap()[:, dcp * 1024:(dcp + 1) * 1024]
                            .rearrange("p (cc t) -> p cc t", cc=2))
                P.op("sync", fst1, waits=[(c.s3Y1, yw)], incs=[(c.s3SO, 16)])

                def fst2(gp, tb=tb, dcp=dcp):
                    return gp.dma_start(
                        out=c.Y1b.ap()[dcp * 256:(dcp + 1) * 256,
                                       tb * 512:(tb + 1) * 512]
                            .rearrange("(cc p) t -> p cc t", p=128),
                        in_=y1.ap()[:, dcp * 1024:(dcp + 1) * 1024]
                            .rearrange("p (cc t) -> p cc t", cc=2))
                P.op("gpsimd", fst2, waits=[(c.s3Y1, yw)], incs=[(c.s3SB, 16)])

        P.wait("sync", c.s3SO, c.s3SO.n)
        P.wait("gpsimd", c.s3SB, c.s3SB.n)
        P.barrier_inc(c.sBAR)
        P.emit()


def _p4(c, top):
    nc = c.nc
    TP = 512
    NP = len(FFN_TP)
    with ExitStack() as st:
        E = st.enter_context
        y1b = E(nc.sbuf_tensor("y1b", [128, 16 * TP], BF16))
        gated = E(nc.sbuf_tensor("gated", [128, 64 * TP], BF16))
        wg_t = [E(nc.sbuf_tensor(f"wg{i}", [128, 2048], BF16)) for i in range(2)]
        wu_t = [E(nc.sbuf_tensor(f"wu{i}", [128, 2048], BF16)) for i in range(2)]
        wdc_t = [E(nc.sbuf_tensor(f"wdc{i}", [128, 8192], BF16)) for i in range(2)]
        silu = [E(nc.sbuf_tensor(f"silu{i}", [128, TP], F32)) for i in range(2)]
        res = [E(nc.sbuf_tensor(f"res4{i}", [128, TP], F32)) for i in range(2)]
        y2 = [E(nc.sbuf_tensor(f"y24{i}", [128, TP], F32)) for i in range(2)]
        sqb = [E(nc.sbuf_tensor(f"sq4{i}", [128, TP], F32)) for i in range(2)]
        nrm = [E(nc.sbuf_tensor(f"nrm4{i}", [128, TP], F32)) for i in range(2)]
        osb = [E(nc.sbuf_tensor(f"osb4{i}", [128, TP], F32)) for i in range(2)]
        tmpb = [E(nc.sbuf_tensor(f"tmp4{i}", [128, TP], F32)) for i in range(2)]
        rows = E(nc.sbuf_tensor("rows4", [1, 4 * 512], F32))
        ps_g = [E(nc.psum_tensor(f"ps_g{i}", [128, 512], F32)) for i in range(2)]
        ps_u = [E(nc.psum_tensor(f"ps_u{i}", [128, 512], F32)) for i in range(2)]
        ps_d = ps_g
        ps_sum = E(nc.psum_tensor("ps_sum4", [128, 512], F32))
        ps_ssq = E(nc.psum_tensor("ps_ssq4", [128, 512], F32))
        ps_mu = E(nc.psum_tensor("ps_mu4", [128, 512], F32))
        ps_rs = E(nc.psum_tensor("ps_rs4", [128, 512], F32))

        P = Phase(nc)
        P.barrier_wait(c.sBAR, 15)

        def emit_bc(np_):
            def fbc(t):
                t.matmul(ps_mu[:, :], c.onesr[:, :], rows[:, 0:512],
                         start=True, stop=True)
                return t.matmul(ps_rs[:, :], c.onesr[:, :], rows[:, 1536:2048],
                                start=True, stop=True)
            w = [(c.s5AC, np_ + 1)]
            if np_ >= 1:
                w.append((c.s5Y, 16 * np_))
            P.op("tensor", fbc, waits=w, incs=[(c.s5MM, 1)])

        def emit_norm(np_, dc):
            ni = np_ * 16 + dc
            t0 = np_ * TP

            def nload(gp):
                return gp.dma_start(
                    out=nrm[ni % 2][:, :],
                    in_=c.Y2f.ap()[dc * 128:(dc + 1) * 128, t0:t0 + TP])
            w = [(c.s4SOs[dc % 2], 16 * (np_ * 8 + dc // 2 + 1))]
            if ni >= 2:
                w.append((c.s5Y, ni - 1))
            P.op("gpsimd", nload, waits=w, incs=[(c.s4NLs[ni % 2], 16)])

            def vnorm(v):
                ins = v.tensor_tensor(tmpb[ni % 2][:, :], nrm[ni % 2][:, :],
                                      ps_mu[:, :], ALU.subtract)
                ssync(v, c.sVS, ins)
                return v.tensor_tensor(osb[ni % 2][:, :], tmpb[ni % 2][:, :],
                                       ps_rs[:, :], ALU.mult)
            w = [(c.s4NLs[ni % 2], 16 * (ni // 2 + 1)), (c.s5MM, np_ + 1)]
            if ni >= 2:
                w.append((c.s5SO, 16 * (ni - 1)))
            P.op("vector", vnorm, waits=w, incs=[(c.s5Y, 1)])

            def ostore(gp):
                return gp.dma_start(
                    out=c.out.ap()[dc * 128:(dc + 1) * 128, t0:t0 + TP],
                    in_=osb[ni % 2][:, :])
            P.op("gpsimd", ostore, waits=[(c.s5Y, ni + 1)],
                 incs=[(c.s5SO, 16)])

        tofs = 0
        for pi in range(NP):
            def yload(sy, tofs=tofs):
                return sy.dma_start(
                    out=y1b.ap().rearrange("p (cc t) -> p cc t", cc=16),
                    in_=c.Y1b.ap()[:, tofs:tofs + TP]
                        .rearrange("(cc p) t -> p cc t", p=128))
            w = []
            if pi >= 1:
                w.append((c.s4PE, c.s4PE.n))
            P.op("sync", yload, waits=w, incs=[(c.s4YL, 16)])

            if pi >= 1:
                emit_bc(pi - 1)
            for s in range(64):
                si = pi * 64 + s

                def wload(sy, s=s, si=si):
                    sy.dma_start(
                        out=wg_t[si % 2].ap(), in_=c.wgB.ap()[s]
                    ).then_inc(c.s4WLs[si % 2].h, 16)
                    return sy.dma_start(
                        out=wu_t[si % 2].ap(), in_=c.wuB.ap()[s])
                w = []
                if si < 2 and pi == 0:
                    w.append((c.s4WP, c.s4WP.n))
                if si >= 2:
                    w.append((c.s4PE, 2 * (si - 1)))
                c.s4WLs[si % 2].n += 16
                P.op("sync", wload, waits=w, incs=[(c.s4WLs[si % 2], 16)])

                def fgate(t, s=s, si=si):
                    for cc in range(16):
                        ins = t.matmul(ps_g[si % 2][:, :],
                                       wg_t[si % 2][:, cc * 128:(cc + 1) * 128],
                                       y1b[:, cc * TP:(cc + 1) * TP],
                                       start=(cc == 0), stop=(cc == 15))
                    return ins
                w = [(c.s4WLs[si % 2], 32 * (si // 2 + 1)),
                     (c.s4YL, 16 * (pi + 1))]
                if si >= 2:
                    w.append((c.s4DV, si - 1))
                if pi >= 1 and s < 2:
                    w.append((c.s4DV2, 16 * pi))
                P.op("tensor", fgate, waits=w, incs=[(c.s4PE, 1)])

                def fup(t, s=s, si=si):
                    for cc in range(16):
                        ins = t.matmul(ps_u[si % 2][:, :],
                                       wu_t[si % 2][:, cc * 128:(cc + 1) * 128],
                                       y1b[:, cc * TP:(cc + 1) * TP],
                                       start=(cc == 0), stop=(cc == 15))
                    return ins
                P.op("tensor", fup, incs=[(c.s4PE, 1)])

                def fsilu(s_, si=si):
                    return s_.activation(silu[si % 2][:, :],
                                         ps_g[si % 2][:, :], AF.Silu)
                P.op("scalar", fsilu, waits=[(c.s4PE, 2 * si + 1)],
                     incs=[(c.s4AC, 1)])

                def fmul(v, s=s, si=si):
                    return v.tensor_tensor(gated[:, s * TP:(s + 1) * TP],
                                           ps_u[si % 2][:, :],
                                           silu[si % 2][:, :], ALU.mult)
                w = [(c.s4PE, 2 * si + 2), (c.s4AC, c.s4AC.n)]
                if pi >= 1:
                    w.append((c.s4PE2, 16 * pi))
                P.op("vector", fmul, waits=w, incs=[(c.s4DV, 1)])

                if pi >= 1 and s % 4 == 1:
                    emit_norm(pi - 1, s // 4)

            gateup_done = c.s4DV.n
            stats_q = []
            for dc in range(16):
                di = pi * 16 + dc

                def wdload(sy, dc=dc, di=di):
                    return sy.dma_start(
                        out=wdc_t[di % 2].ap(), in_=c.wdB.ap()[dc])
                w = []
                if di >= 2:
                    w.append((c.s4PE2, di - 1))
                P.op("sync", wdload, waits=w, incs=[(c.s4WL2s[di % 2], 16)])

                def fdown(t, dc=dc, di=di):
                    for s in range(64):
                        ins = t.matmul(ps_d[di % 2][:, :],
                                       wdc_t[di % 2][:, s * 128:(s + 1) * 128],
                                       gated[:, s * TP:(s + 1) * TP],
                                       start=(s == 0), stop=(s == 63))
                    return ins
                w = [(c.s4WL2s[di % 2], 16 * (di // 2 + 1)),
                     (c.s4DV, gateup_done)]
                if di >= 2:
                    w.append((c.s4DV2, di - 1))
                P.op("tensor", fdown, waits=w, incs=[(c.s4PE2, 1)])
                while stats_q and stats_q[0][0] <= dc - 2:
                    _, fn, w2 = stats_q.pop(0)
                    P.op("tensor", fn, waits=w2, incs=[(c.s5ST, 2)])

                def fres(sy, dc=dc, di=di, tofs=tofs):
                    return sy.dma_start(
                        out=res[di % 2][:, :],
                        in_=c.Y1f.ap()[dc * 128:(dc + 1) * 128, tofs:tofs + TP])
                w = []
                if di >= 2:
                    w.append((c.s4DV2, di - 1))
                P.op("sync", fres, waits=w, incs=[(c.s4RLs[di % 2], 16)])

                def fev(v, dc=dc, di=di):
                    return v.tensor_tensor(y2[di % 2][:, :],
                                           ps_d[di % 2][:, :],
                                           res[di % 2][:, :], ALU.add)
                w = [(c.s4PE2, di + 1),
                     (c.s4RLs[di % 2], 16 * (di // 2 + 1))]
                if di >= 2:
                    w.append((c.s4SOs[di % 2], 16 * (di // 2)))
                if dc >= 2:
                    w.append((c.s5ST, 32 * pi + 2 * (dc - 1)))
                elif pi >= 1:
                    w.append((c.s5ST, 32 * pi))
                P.op("vector", fev, waits=w, incs=[(c.s4DV2, 1)])

                def fsq(v, dc=dc, di=di):
                    return v.tensor_tensor(sqb[dc % 2][:, :],
                                           y2[di % 2][:, :],
                                           y2[di % 2][:, :], ALU.mult)
                w = [(c.s4DV2, di + 1)]
                if dc >= 2:
                    w.append((c.s5ST, 32 * pi + 2 * (dc - 1)))
                P.op("vector", fsq, waits=w, incs=[(c.s5SQ, 1)])

                def fstat(t, dc=dc, di=di):
                    t.matmul(ps_sum[0:1, :], c.onesc[:, :], y2[di % 2][:, :],
                             start=(dc == 0), stop=(dc == 15))
                    return t.matmul(ps_ssq[0:1, :], c.onesc[:, :],
                                    sqb[dc % 2][:, :],
                                    start=(dc == 0), stop=(dc == 15))
                w2 = [(c.s5SQ, 16 * pi + dc + 1)]
                if dc == 0 and pi >= 1:
                    w2.append((c.s5SM, pi))
                stats_q.append((dc, fstat, w2))

                def fst(sy, dc=dc, di=di, tofs=tofs):
                    return sy.dma_start(
                        out=c.Y2f.ap()[dc * 128:(dc + 1) * 128, tofs:tofs + TP],
                        in_=y2[di % 2][:, :])
                P.op("sync", fst, waits=[(c.s4DV2, di + 1)],
                     incs=[(c.s4SOs[di % 2], 16)])
            while stats_q:
                _, fn, w2 = stats_q.pop(0)
                P.op("tensor", fn, waits=w2, incs=[(c.s5ST, 2)])

            def fsm(v, pi=pi):
                v.tensor_scalar_mul(rows[:, 0:512], ps_sum[0:1, :], 1.0 / D)
                ins = v.tensor_scalar_mul(rows[:, 512:1024], ps_ssq[0:1, :],
                                          1.0 / D)
                ssync(v, c.sVS, ins)
                ins = v.tensor_tensor(rows[:, 1024:1536], rows[:, 0:512],
                                      rows[:, 0:512], ALU.mult)
                ssync(v, c.sVS, ins)
                return v.tensor_tensor(rows[:, 1024:1536], rows[:, 512:1024],
                                       rows[:, 1024:1536], ALU.subtract)
            w = [(c.s5ST, 32 * (pi + 1))]
            if pi >= 1:
                w.append((c.s5MM, pi))
            P.op("vector", fsm, waits=w, incs=[(c.s5SM, 1)])

            def fac(s, pi=pi):
                ins = s.activation(rows[:, 1536:2048], rows[:, 1024:1536], AF.Ln,
                                   bias=c.cvec_sb[0:1, 1:2])
                ssync(s, c.sSS, ins)
                return s.activation(rows[:, 1536:2048], rows[:, 1536:2048],
                                    AF.Exp, scale=-0.5)
            P.op("scalar", fac, waits=[(c.s5SM, pi + 1)], incs=[(c.s5AC, 1)])
            tofs += TP

        emit_bc(NP - 1)
        for dc in range(16):
            emit_norm(NP - 1, dc)

        P.wait("gpsimd", c.s5SO, c.s5SO.n)
        P.wait("sync", c.s5SO, c.s5SO.n)
        P.barrier_inc(c.sBAR)
        P.emit()


def _host_inputs(inputs, debug_taps=()):
    x = np.asarray(inputs["L0_attn_x"], dtype=np.float32)
    Wq = np.asarray(inputs["L0_attn_Wq"], dtype=np.float32)
    Wk = np.asarray(inputs["L0_attn_Wk"], dtype=np.float32)
    Wv = np.asarray(inputs["L0_attn_Wv"], dtype=np.float32)
    Wo = np.asarray(inputs["L0_attn_Wo"], dtype=np.float32)
    lamb = np.asarray(inputs["L0_attn_lambda_init"], dtype=np.float32)
    wg = np.asarray(inputs["L0_ffn_w_gate"], dtype=np.float32)
    wu = np.asarray(inputs["L0_ffn_w_up"], dtype=np.float32)
    wd = np.asarray(inputs["L0_ffn_w_down"], dtype=np.float32)

    p = np.arange(128)[:, None]
    q = np.arange(128)[None, :]
    maskc = np.where(p > q, np.float32(NEG), np.float32(0.0)).astype(np.float32)
    ones_row = np.ones((1, 128), dtype=np.float32)
    ones_col = np.ones((128, 1), dtype=np.float32)
    idt = np.eye(128, dtype=np.float32)
    cvec = np.zeros((128, 2), dtype=np.float32)
    cvec[:, 0] = EBIAS
    cvec[:, 1] = LN_EPS
    wgT = np.ascontiguousarray(wg.T)
    wuT = np.ascontiguousarray(wu.T)
    wdT = np.ascontiguousarray(wd.T)
    Wo_c = np.ascontiguousarray(Wo)

    in_maps = []
    for core in range(8):
        b, half = core // 2, core % 2
        xT = np.ascontiguousarray(x[b].T)
        m = {
            "xT": xT,
            "wq": np.ascontiguousarray(Wq[:, half * 1024:(half + 1) * 1024]),
            "wk": np.ascontiguousarray(Wk[:, half * 1024:(half + 1) * 1024]),
            "wv": np.ascontiguousarray(Wv[:, half * 1024:(half + 1) * 1024]),
            "wo": Wo_c,
            "lam": np.ascontiguousarray(lamb[half * 8:(half + 1) * 8][None, :]),
            "wgT": wgT, "wuT": wuT, "wdT": wdT,
            "maskc": maskc, "ones_row": ones_row, "ones_col": ones_col,
            "idt": idt, "cvec": cvec,
            "xresT": np.ascontiguousarray(xT[:, half * TOK:(half + 1) * TOK]),
        }
        in_maps.append(m)
    return in_maps


def kernel(**inputs):
    nc = bass.Bass("TRN2", target_bir_lowering=False)
    build(nc, DEBUG_TAPS, PHASE_LIMIT)
    in_maps = _host_inputs(inputs, DEBUG_TAPS)
    res = run_bass_kernel_spmd(nc, in_maps, core_ids=list(range(8)))
    out = np.empty((B, T, D), dtype=np.float32)
    for core in range(8):
        b, half = core // 2, core % 2
        o = np.asarray(res.results[core]["out"], dtype=np.float32)
        out[b, half * TOK:(half + 1) * TOK, :] = o.T
    return out
